# version 12
# baseline (speedup 1.0000x reference)
import contextlib
import numpy as np

import concourse.bass as bass
import concourse.mybir as mybir
from concourse.bass_utils import run_bass_kernel_spmd

F32 = mybir.dt.float32
BF16 = mybir.dt.bfloat16
AF = mybir.ActivationFunctionType
OP = mybir.AluOpType
AX = mybir.AxisListType

D = 1024
NH = 8
QKVD = 3072
FF = 2816
MODD = 6144
PI = float(np.pi)
TWO_PI = float(2.0 * np.pi)
ENGINES = ("sync", "tensor", "vector", "scalar", "gpsimd")


class Sched:
    def __init__(self):
        self.steps = {e: [] for e in ENGINES}
        self.count = {e: 0 for e in ENGINES}
        self.slot_count = {}

    def add(self, eng, deps, fn):
        deps = [d for d in deps if d is not None]
        self.steps[eng].append((deps, fn, False))
        self.count[eng] += 1
        return (eng, self.count[eng])

    def dma(self, deps, fn, slot="misc"):
        deps = [d for d in deps if d is not None]
        key = "dslot:" + slot
        self.steps["sync"].append((deps, fn, key))
        self.count["sync"] += 1
        self.slot_count[slot] = self.slot_count.get(slot, 0) + 1
        return (key, 16 * self.slot_count[slot])

    def dma_total(self, slot):
        return ("dtot:" + slot, 0)

    def all_dma_done(self):
        return [("dtot:" + s, 0) for s in self.slot_count]

    def emit(self, block, sems, slot_sems):
        needed = {e: set() for e in ENGINES}
        for eng in ENGINES:
            for deps, fn, is_dma in self.steps[eng]:
                for key, val in deps:
                    if key in needed:
                        needed[key].add(val)
        remap = {}
        for eng in ENGINES:
            cnt = 0
            for i in range(1, len(self.steps[eng]) + 1):
                if i in needed[eng]:
                    cnt += 1
                remap[(eng, i)] = cnt
        hwm = {}

        def mk(eng):
            def _body(b):
                ordinal = 0
                for deps, fn, is_dma in self.steps[eng]:
                    ordinal += 1
                    for key, val in deps:
                        if key == eng and val >= ordinal:
                            continue
                        if key in needed:
                            sem, v = sems[key], remap[(key, val)]
                            hkey = key
                        elif key.startswith("dslot:"):
                            sem, v, hkey = slot_sems[key[6:]], val, key
                        else:
                            s = key[5:]
                            sem = slot_sems[s]
                            v = 16 * self.slot_count[s]
                            hkey = "dslot:" + s
                        if v <= hwm.get((eng, hkey), 0):
                            continue
                        hwm[(eng, hkey)] = v
                        b.wait_ge(sem, v)
                    if is_dma:
                        fn(b).then_inc(slot_sems[is_dma[6:]], 16)
                    else:
                        r = fn(b)
                        if r is not None and ordinal in needed[eng]:
                            r.then_inc(sems[eng], 1)
            return _body

        for eng in ENGINES:
            if self.steps[eng]:
                getattr(block, eng)(mk(eng))


def build_program(ntiles, debug=(), stop_after="b"):
    _PH = ["prologue", "wprep", "a1", "a15", "a2", "b"]
    _lvl = _PH.index(stop_after)
    nc = bass.Bass("TRN2", target_bir_lowering=False,
                   detect_race_conditions=False)
    TOK = ntiles * 128
    assert ntiles % 4 == 0
    NG = ntiles // 4

    x_hbm = nc.dram_tensor("x", [TOK, D], F32, kind="ExternalInput").ap()
    wqkv_hbm = nc.dram_tensor("wqkv", [QKVD, D], F32, kind="ExternalInput").ap()
    bqkv_hbm = nc.dram_tensor("bqkv", [1, QKVD], F32, kind="ExternalInput").ap()
    wout_hbm = nc.dram_tensor("wout", [D, D], F32, kind="ExternalInput").ap()
    bout_hbm = nc.dram_tensor("bout", [1, D], F32, kind="ExternalInput").ap()
    qw00_hbm = nc.dram_tensor("qw00", [1, 4], F32, kind="ExternalInput").ap()
    phases_hbm = nc.dram_tensor("phases", [16, D], F32, kind="ExternalInput").ap()
    gates_hbm = nc.dram_tensor("gates", [1, 96], F32, kind="ExternalInput").ap()
    sup_hbm = nc.dram_tensor("sup", [1, 4], F32, kind="ExternalInput").ap()
    wgate_hbm = nc.dram_tensor("wgate", [FF, D], F32, kind="ExternalInput").ap()
    wup_hbm = nc.dram_tensor("wup", [FF, D], F32, kind="ExternalInput").ap()
    wdown_hbm = nc.dram_tensor("wdown", [D, FF], F32, kind="ExternalInput").ap()
    c_hbm = nc.dram_tensor("c_col", [D, 1], F32, kind="ExternalInput").ap()
    wmod_hbm = nc.dram_tensor("wmod", [D, MODD], F32, kind="ExternalInput").ap()

    y_hbm = nc.dram_tensor("y", [TOK, D], F32, kind="ExternalOutput").ap()
    qkv_park = nc.dram_tensor("qkv_park", [TOK, QKVD], F32).ap()
    xn_park = nc.dram_tensor("xn_park", [TOK, 2048], BF16).ap()
    x2_park = nc.dram_tensor("x2_park", [TOK, D], F32).ap()
    rowrt = nc.dram_tensor("rowrt", [1, 2048], F32).ap()
    wq_park = {
        "g": nc.dram_tensor("wgq_park", [2, D, FF], BF16).ap(),
        "u": nc.dram_tensor("wuq_park", [2, D, FF], BF16).ap(),
        "d": nc.dram_tensor("wdq_park", [2, FF, D], BF16).ap(),
    }

    dbg = {}

    def dbg_out(name, shape, dtype=F32):
        if name in debug:
            dbg[name] = nc.dram_tensor("dbg_" + name, list(shape), dtype,
                                       kind="ExternalOutput").ap()
        return dbg.get(name)

    sch = Sched()
    bank_w = [None] * 8
    bank_r = [None] * 8

    def pe_w(bank, deps, fn):
        tok = sch.add("tensor", list(deps) + [bank_r[bank]], fn)
        bank_w[bank] = tok
        return tok

    def rd_bank(eng, bank, deps, fn):
        tok = sch.add(eng, list(deps) + [bank_w[bank]], fn)
        bank_r[bank] = tok
        return tok

    ctx = contextlib.ExitStack()
    with ctx:
        consts = ctx.enter_context(nc.sbuf_tensor("consts", [128, 9216], F32)).ap()
        PF = ctx.enter_context(nc.sbuf_tensor("pf", [128, 28672], F32)).ap()
        PB = ctx.enter_context(nc.sbuf_tensor("pbf", [128, 28672], BF16)).ap()
        ident = ctx.enter_context(nc.sbuf_tensor("ident", [128, 128], F32)).ap()
        pb = [ctx.enter_context(nc.psum_tensor(f"pb{i}", [128, 512], F32)).ap()
              for i in range(8)]
        sems = {e: ctx.enter_context(nc.semaphore("sem_" + e)) for e in ENGINES}

        CO = {}
        _c = [0]
        for nm, n in [("cos", 128), ("sin", 128), ("ga", 1024), ("gb", 1024),
                      ("cb1", 1024), ("sb_", 1024), ("cols", 256), ("ones", 128),
                      ("b_ln", 128), ("rows", 4352)]:
            CO[nm] = _c[0]
            _c[0] += n
        assert _c[0] <= 9216

        def cv(name, n):
            return consts[:, CO[name]:CO[name] + n]

        COLS = {}
        _cc = [CO["cols"]]
        for nm, n in [("ca1", 8), ("sa", 8), ("A", 21), ("B", 21), ("C", 21),
                      ("gam", 4), ("supw", 4), ("silc", 8)]:
            COLS[nm] = _cc[0]
            _cc[0] += n
        assert _cc[0] <= CO["cols"] + 256

        def col(name, n=1, j=0):
            return consts[:, COLS[name] + j:COLS[name] + j + n]

        ROWS0 = CO["rows"]

        def rwc(off, n):
            assert off + n <= 4352
            return consts[0:1, ROWS0 + off:ROWS0 + off + n]
        biasp_row = rwc(0, QKVD)
        bout_row = rwc(QKVD, D)
        ones_row = consts[0:1, CO["ones"]:CO["ones"] + 128]

        def pr(off, n):
            assert off + n <= 8192
            return PF[0:1, off:off + n]

        dummy_row = rwc(4128, 128)
        t_dz = sch.add("gpsimd", [], lambda g: g.memset(dummy_row, 0.0))
        t_pre1 = sch.add("scalar", [t_dz], lambda s: s.preload_act_table(AF.Silu))
        for _ in range(3):
            t_pre1 = sch.add("scalar", [t_pre1], lambda s: s.activation(
                dummy_row, dummy_row, AF.Silu))
        t_id0 = sch.add("gpsimd", [], lambda g: g.memset(ident[:], 1.0))
        t_id = sch.add("gpsimd", [t_id0], lambda g: g.affine_select(
            out=ident[:], in_=ident[:], compare_op=OP.is_equal, fill=0.0,
            base=0, pattern=[[-1, 128]], channel_multiplier=1))
        t_ones = sch.add("gpsimd", [], lambda g: g.memset(ones_row, 1.0))

        sch.dma([], lambda b: b.dma_start(
            col("silc", 8), bass.AP(c_hbm.tensor, 0, [[1, 128], [128, 8]]),
            allow_slow_non_contiguous=True), slot="misc")
        for r in range(16):
            sch.dma([], lambda b, r=r: b.dma_start(
                pr(0, 128)[:, r:128:16], phases_hbm[r:r + 1, r:r + 128:16],
                allow_slow_non_contiguous=True), slot="misc")
        sch.dma([], lambda b: b.dma_start(pr(128, 96), gates_hbm), slot="misc")
        sch.dma([], lambda b: b.dma_start(pr(224, 4), qw00_hbm), slot="misc")
        sch.dma([], lambda b: b.dma_start(pr(228, 4), sup_hbm), slot="misc")
        sch.dma([], lambda b: b.dma_start(biasp_row, bqkv_hbm), slot="misc")
        sch.dma([], lambda b: b.dma_start(bout_row, bout_hbm), slot="misc")
        d_c = d_ph = d_gates = d_qw = d_sup = d_bq = d_bo = sch.dma_total("misc")

        t_silc = sch.add("scalar", [d_c], lambda s: s.activation(
            col("silc", 8), col("silc", 8), AF.Silu))

        TP1 = float(np.float32(TWO_PI))
        TP2 = float(TWO_PI - TP1)
        HP1 = float(np.float32(PI / 2))
        HP2 = float(PI / 2 - HP1)

        def emit_wrap(dst, src, n, half_shift):
            if half_shift:
                tok = sch.add("vector", [d_ph], lambda v: v.tensor_scalar(
                    pr(dst, n), pr(src, n), HP1, None, OP.add))
                tok = sch.add("vector", [tok], lambda v: v.tensor_scalar(
                    pr(dst, n), pr(dst, n), HP2, None, OP.add))
            else:
                tok = sch.add("vector", [d_ph], lambda v: v.tensor_copy(
                    pr(dst, n), pr(src, n)))
            for _ in range(8):
                sch.add("vector", [], lambda v: v.tensor_scalar(
                    pr(512, n), pr(dst, n), PI, None, OP.is_gt))
                sch.add("vector", [], lambda v: v.scalar_tensor_tensor(
                    pr(dst, n), pr(512, n), -TP1, pr(dst, n), OP.mult, OP.add))
                sch.add("vector", [], lambda v: v.scalar_tensor_tensor(
                    pr(dst, n), pr(512, n), -TP2, pr(dst, n), OP.mult, OP.add))
                sch.add("vector", [], lambda v: v.tensor_scalar(
                    pr(512, n), pr(dst, n), -PI, None, OP.is_lt))
                sch.add("vector", [], lambda v: v.scalar_tensor_tensor(
                    pr(dst, n), pr(512, n), TP1, pr(dst, n), OP.mult, OP.add))
                tok = sch.add("vector", [], lambda v: v.scalar_tensor_tensor(
                    pr(dst, n), pr(512, n), TP2, pr(dst, n), OP.mult, OP.add))
            return tok

        t_w1 = emit_wrap(256, 0, 128, True)
        t_w2 = emit_wrap(384, 0, 128, False)
        t_cosr = sch.add("scalar", [t_w1], lambda s: s.activation(
            pr(256, 128), pr(256, 128), AF.Sin))
        t_sinr = sch.add("scalar", [t_w2], lambda s: s.activation(
            pr(384, 128), pr(384, 128), AF.Sin))

        gates4 = pr(128, 96).rearrange("o (d i f) -> o d i f", d=3, i=8)
        t_th = sch.add("vector", [d_gates], lambda v: v.tensor_scalar(
            pr(640, 21).rearrange("o (d i f) -> o d i f", d=3, i=7),
            gates4[:, :, 0:7, 0:1], 0.5, None, OP.mult))
        t_phi = sch.add("vector", [d_gates], lambda v: v.tensor_copy(
            pr(672, 21).rearrange("o (d i f) -> o d i f", d=3, i=7),
            gates4[:, :, 0:7, 1:2]))
        t_ct0 = sch.add("vector", [t_th], lambda v: v.tensor_scalar(
            pr(704, 21), pr(640, 21), HP1, None, OP.add))
        t_ct0 = sch.add("vector", [t_ct0], lambda v: v.tensor_scalar(
            pr(704, 21), pr(704, 21), HP2, None, OP.add))
        t_ct = sch.add("scalar", [t_ct0], lambda s: s.activation(
            pr(704, 21), pr(704, 21), AF.Sin))
        t_st = sch.add("scalar", [t_th], lambda s: s.activation(
            pr(736, 21), pr(640, 21), AF.Sin))
        t_cp0 = sch.add("vector", [t_phi], lambda v: v.tensor_scalar(
            pr(768, 21), pr(672, 21), HP1, None, OP.add))
        t_cp0 = sch.add("vector", [t_cp0], lambda v: v.tensor_scalar(
            pr(768, 21), pr(768, 21), HP2, None, OP.add))
        t_cp = sch.add("scalar", [t_cp0], lambda s: s.activation(
            pr(768, 21), pr(768, 21), AF.Sin))
        t_sp = sch.add("scalar", [t_phi], lambda s: s.activation(
            pr(800, 21), pr(672, 21), AF.Sin))
        t_a1 = sch.add("vector", [t_ct, t_st, t_cp, t_sp],
                       lambda v: v.tensor_tensor(pr(832, 21), pr(736, 21),
                                                 pr(768, 21), OP.mult))
        t_B = sch.add("vector", [t_sp], lambda v: v.tensor_tensor(
            pr(864, 21), pr(736, 21), pr(800, 21), OP.mult))
        t_A = sch.add("vector", [t_a1], lambda v: v.scalar_tensor_tensor(
            pr(896, 21), pr(704, 21), -1.0, pr(832, 21), OP.add, OP.add))
        t_C0 = sch.add("vector", [t_a1], lambda v: v.tensor_tensor(
            pr(928, 21), pr(704, 21), pr(832, 21), OP.subtract))
        t_C = sch.add("vector", [t_C0], lambda v: v.tensor_scalar(
            pr(928, 21), pr(928, 21), -1.0, None, OP.add))

        t_rat = sch.add("scalar", [d_qw], lambda s: s.activation(
            pr(960, 4), pr(224, 4), AF.Sin))
        R31 = float(np.float32(0.3))
        R32 = float(0.3 - R31)
        t_r2 = sch.add("vector", [t_rat], lambda v: v.tensor_scalar(
            pr(980, 4), pr(960, 4), R32, None, OP.mult))
        t_r2 = sch.add("vector", [t_r2], lambda v: v.tensor_scalar(
            pr(960, 4), pr(960, 4), R31, 0.5, OP.mult, OP.add))
        t_r2 = sch.add("vector", [t_r2], lambda v: v.tensor_tensor(
            pr(960, 4), pr(960, 4), pr(980, 4), OP.add))
        t_1m = sch.add("vector", [t_r2], lambda v: v.tensor_scalar(
            pr(964, 4), pr(960, 4), -1.0, 1.0, OP.mult, OP.add))
        t_g0 = sch.add("vector", [t_1m], lambda v: v.tensor_tensor(
            pr(968, 4), pr(960, 4), pr(964, 4), OP.mult))
        t_gam = sch.add("vector", [t_g0], lambda v: v.tensor_scalar(
            pr(968, 4), pr(968, 4), float(128 ** -0.5), None, OP.mult))

        def load_lnexp_set(s):
            return s.add_instruction(mybir.InstLoadActFuncSet(
                name=s.bass.get_next_instruction_name(),
                act_func_set_id=6, ins=[], outs=[]))
        t_pre2 = sch.add("scalar", [t_rat, t_cosr, t_sinr, t_ct, t_st,
                                    t_cp, t_sp, t_silc], load_lnexp_set)
        for _ in range(3):
            t_pre2 = sch.add("scalar", [t_pre2], lambda s: s.activation(
                dummy_row, dummy_row, AF.Exp))
        t_se = sch.add("scalar", [d_sup, t_pre2],
                       lambda s: s.activation(pr(972, 4), pr(228, 4), AF.Exp))
        t_ssum = sch.add("vector", [t_se], lambda v: v.tensor_reduce(
            pr(976, 1), pr(972, 4), AX.X, OP.add))
        t_sr = sch.add("vector", [t_ssum], lambda v: v.reciprocal(
            pr(977, 1), pr(976, 1)))
        t_sw = sch.add("vector", [t_sr], lambda v: v.tensor_scalar(
            pr(972, 4), pr(972, 4), pr(977, 1), None, OP.mult))

        def bcast(dst, src_row, deps):
            n = src_row.shape[-1]
            tok = None
            for o in range(0, n, 512):
                w = min(512, n - o)
                mt = pe_w(7, list(deps) + [t_ones],
                          lambda t, o=o, w=w: t.matmul(
                              pb[7][:, 0:w], ones_row, src_row[:, o:o + w],
                              start=True, stop=True))
                tok = rd_bank("scalar", 7, [], lambda s, o=o, w=w: s.activation(
                    dst[:, o:o + w], pb[7][:, 0:w], AF.Copy))
            return tok

        bcast(cv("cos", 128), pr(256, 128), [t_cosr])
        bcast(cv("sin", 128), pr(384, 128), [t_sinr])
        t_bcA = bcast(col("A", 21), pr(896, 21), [t_A])
        t_bcB = bcast(col("B", 21), pr(864, 21), [t_B])
        t_bcC = bcast(col("C", 21), pr(928, 21), [t_C])
        t_bcgam = bcast(col("gam", 4), pr(968, 4), [t_gam])
        t_bcsup = bcast(col("supw", 4), pr(972, 4), [t_sw])

        MODR = 1024
        WMB = [8192, 8192 + 4096]

        def wmb(i, n):
            return PF[:, WMB[i]:WMB[i] + n]

        mod_parts = []
        last_mm = [None, None]
        for c0, nw in [(0, 8), (4096, 4)]:
            for kc in range(8):
                def dl(b, kc=kc, c0=c0, nw=nw):
                    return b.dma_start(wmb(kc % 2, nw * 512),
                                       wmod_hbm[kc * 128:(kc + 1) * 128,
                                                c0:c0 + nw * 512])
                dtok = sch.dma([t_silc, last_mm[kc % 2]], dl,
                               slot="wm%d" % (kc % 2))

                def mmstep(t, kc=kc, nw=nw):
                    last = None
                    for j in range(nw):
                        last = t.matmul(pb[j][0:1, :], col("silc", 8)[:, kc:kc + 1],
                                        wmb(kc % 2, nw * 512)[:, j * 512:(j + 1) * 512],
                                        start=(kc == 0), stop=(kc == 7))
                    return last
                deps = [dtok]
                if kc == 0:
                    deps += [bank_r[j] for j in range(nw)]
                tok = sch.add("tensor", deps, mmstep)
                for j in range(nw):
                    bank_w[j] = tok
                last_mm[kc % 2] = tok
            for j in range(nw):
                mod_parts.append(rd_bank("vector", j, [], lambda v, j=j, c0=c0:
                                 v.tensor_copy(pr(MODR + c0 + j * 512, 512),
                                               pb[j][0:1, :])))
        t_mod = mod_parts[-1]
        t_mod_all = sch.add("vector", mod_parts, lambda v: v.memset(consts[0:1, CO["rows"] + 4200:CO["rows"] + 4201], 0.0))

        if dbg_out("mod", (1, MODD)):
            sch.dma([t_mod_all], lambda b: b.dma_start(dbg["mod"], pr(MODR, MODD)),
                    slot="dbg1")

        def mseg(i):
            return pr(MODR + i * 1024, 1024)

        sgn_row = PF[0:1, 8192:8192 + 1024]
        abs_row = PF[0:1, 9216:9216 + 1024]
        prow_bc_toks = []
        prev_bc = None
        for i, nm in ((2, "ga"), (5, "gb")):
            ts1 = sch.add("vector", [t_mod_all, prev_bc], lambda v, i=i:
                          v.tensor_scalar(sgn_row, mseg(i), 0.0, 2.0,
                                          OP.is_ge, OP.mult))
            ts2 = sch.add("vector", [ts1], lambda v: v.tensor_scalar(
                sgn_row, sgn_row, -1.0, None, OP.add))
            ta = sch.add("scalar", [t_mod_all, prev_bc], lambda s, i=i:
                         s.activation(abs_row, mseg(i), AF.Abs))
            tm = sch.add("vector", [ta], lambda v: v.tensor_scalar(
                abs_row, abs_row, 0.25, None, OP.max))
            tg = sch.add("vector", [ts2, tm], lambda v: v.tensor_tensor(
                sgn_row, sgn_row, abs_row, OP.mult))
            prev_bc = bcast(cv(nm, 1024), sgn_row, [tg])
            prow_bc_toks.append(prev_bc)

        t_cb1 = sch.add("vector", [t_mod_all, prev_bc], lambda v: v.tensor_scalar(
            abs_row, mseg(4), 1.0, None, OP.add))
        prow_bc_toks.append(bcast(cv("cb1", 1024), abs_row, [t_cb1]))
        prow_bc_toks.append(bcast(cv("sb_", 1024), mseg(3), [t_mod_all]))

        t_ca1r = sch.add("vector", [t_mod_all], lambda v: v.tensor_scalar(
            pr(2048, 1024), mseg(1), 1.0, None, OP.add))
        d_w1 = sch.dma([t_ca1r], lambda b: b.dma_start(
            rowrt[:, 0:1024], pr(2048, 1024)), slot="rw1")
        d_r1 = sch.dma([d_w1], lambda b: b.dma_start(
            col("ca1", 8), bass.AP(rowrt.tensor, 0, [[1, 128], [128, 8]]),
            allow_slow_non_contiguous=True), slot="rr1")
        d_w2 = sch.dma([t_mod_all], lambda b: b.dma_start(
            rowrt[:, 1024:2048], mseg(0)), slot="rw2")
        d_r2 = sch.dma([d_w2], lambda b: b.dma_start(
            col("sa", 8), bass.AP(rowrt.tensor, 1024, [[1, 128], [128, 8]]),
            allow_slow_non_contiguous=True), slot="rr2")

        prologue_done = sch.add("vector", [d_r1, d_r2, t_se, t_sw, t_bcA, t_bcB,
                                           t_bcC, t_bcgam, t_bcsup, t_mod_all]
                                + prow_bc_toks, lambda v: v.memset(consts[0:1, CO["rows"] + 4200:CO["rows"] + 4201], 0.0))

        def emit_quant(src, dst, n, scr, deps):
            nb = n // 16
            zn, az, acc, sg, am, rs = (scr["zn"][:, 0:n], scr["az"][:, 0:n],
                                       scr["acc"][:, 0:n], scr["sg"][:, 0:n],
                                       scr["am"][:, 0:nb], scr["rs"][:, 0:nb])

            def bview(a):
                return a.rearrange("p (b e) -> p e b", e=16)

            def sview(s):
                return s.unsqueeze(1).broadcast_to([128, 16, nb])

            t0 = sch.add("vector", deps, lambda v: v.tensor_reduce(
                am, src.rearrange("p (b e) -> p b e", e=16), AX.X, OP.max,
                apply_absolute_value=True))
            t1 = sch.add("vector", [t0], lambda v: v.tensor_scalar(
                am, am, 6e-12, None, OP.max))
            t2 = sch.add("vector", [t1], lambda v: v.reciprocal(rs, am))
            t3 = sch.add("vector", [t2], lambda v: v.tensor_tensor(
                bview(zn), bview(src), sview(rs), OP.mult))
            t4 = sch.add("scalar", [t3], lambda s: s.activation(az, zn, AF.Abs,
                                                                scale=6.0))
            t5 = sch.add("scalar", [t3], lambda s: s.sign(sg, zn))
            t2b = sch.add("vector", [t3], lambda v: v.tensor_scalar(
                rs, am, 1.0 / 6.0, None, OP.mult))
            tc = sch.add("vector", [t4], lambda v: v.tensor_scalar(
                acc, az, 0.25, None, OP.is_gt))
            for thr in (0.75, 1.25, 1.75, 2.5, 3.5, 5.0):
                tc = sch.add("vector", [tc], lambda v, thr=thr:
                             v.scalar_tensor_tensor(acc, az, float(thr), acc,
                                                    OP.is_gt, OP.add))
            t6 = sch.add("vector", [tc, t5], lambda v: v.tensor_scalar(
                az, acc, -4.0, 0.0, OP.add, OP.max))
            t7 = sch.add("vector", [t6], lambda v: v.tensor_scalar(
                zn, acc, -6.0, 0.0, OP.add, OP.max))
            t8 = sch.add("vector", [t7], lambda v: v.scalar_tensor_tensor(
                acc, acc, 0.5, zn, OP.mult, OP.add))
            t9 = sch.add("vector", [t8], lambda v: v.scalar_tensor_tensor(
                acc, az, 0.5, acc, OP.mult, OP.add))
            ta_ = sch.add("vector", [t9], lambda v: v.tensor_tensor(
                acc, acc, sg, OP.mult))
            tb_ = sch.add("vector", [ta_, t2b], lambda v: v.tensor_tensor(
                bview(dst), bview(acc), sview(rs), OP.mult))
            return tb_

        tp_state = {"i": 0}

        def emit_tp(src_chunk, hi_dst, lo_dst, deps):
            bank = 4 + (tp_state["i"] % 2)
            tp_state["i"] += 1
            ptok = pe_w(bank, deps, lambda t: t.transpose(
                pb[bank][:, 0:128], src_chunk, ident[:]))
            htok = sch.add("scalar", [ptok], lambda s: s.activation(
                hi_dst, pb[bank][:, 0:128], AF.Copy))
            ltok = rd_bank("vector", bank, [htok], lambda v: v.tensor_tensor(
                lo_dst, pb[bank][:, 0:128], hi_dst, OP.subtract))
            return ltok

        WLD = [8192, 8192 + 1024]
        STG = 8192 + 2048

        def wqkv_prep_half(half):
            toks = []
            stg = PF[:, STG:STG + 128]
            prev_stg_readers = []
            bias_ev_prev = None
            for nt in range(12):
                ntg = half * 12 + nt
                dtok = sch.dma([prologue_done] + prev_stg_readers,
                               lambda b, ntg=ntg, nt=nt: b.dma_start(
                                   PF[:, WLD[nt % 2]:WLD[nt % 2] + 1024],
                                   wqkv_hbm[ntg * 128:(ntg + 1) * 128, :]),
                               slot="wld%d" % (nt % 2))
                bb = 6 + nt % 2
                for kc in range(8):
                    ptok = pe_w(4 + kc % 2, [dtok], lambda t, nt=nt, kc=kc:
                                t.transpose(pb[4 + kc % 2][:, 0:128],
                                            PF[:, WLD[nt % 2] + kc * 128:
                                               WLD[nt % 2] + kc * 128 + 128],
                                            ident[:]))
                    etok = rd_bank("scalar", 4 + kc % 2, prev_stg_readers,
                                   lambda s, kc=kc: s.activation(
                                       stg, pb[4 + kc % 2][:, 0:128], AF.Copy))
                    btok = pe_w(bb, [etok, d_r2], lambda t, kc=kc, nt=nt, bb=bb:
                                t.matmul(pb[bb][0:1, 0:128],
                                         col("sa", 8)[:, kc:kc + 1], stg,
                                         start=(kc == 0), stop=(kc == 7)))
                    hoff = kc * 1536 + nt * 128
                    hitok = sch.add("vector", [etok, d_r1], lambda v, hoff=hoff, kc=kc:
                                    v.tensor_scalar(PB[:, hoff:hoff + 128], stg,
                                                    col("ca1", 8)[:, kc:kc + 1],
                                                    None, OP.mult))
                    lotok = sch.add("vector", [hitok], lambda v, hoff=hoff, kc=kc:
                                    v.scalar_tensor_tensor(
                                        PB[:, 12288 + hoff:12288 + hoff + 128],
                                        stg, col("ca1", 8)[:, kc:kc + 1],
                                        PB[:, hoff:hoff + 128],
                                        OP.mult, OP.subtract))
                    prev_stg_readers = [btok, lotok]
                bev = rd_bank("vector", bb, [d_bq], lambda v, ntg=ntg, bb=bb:
                              v.scalar_tensor_tensor(
                                  biasp_row[:, ntg * 128:ntg * 128 + 128],
                                  pb[bb][0:1, 0:128], 1.0,
                                  biasp_row[:, ntg * 128:ntg * 128 + 128],
                                  OP.mult, OP.add))
                toks.append(bev)
                toks.append(prev_stg_readers[1])
            return sch.add("vector", toks, lambda v: v.memset(consts[0:1, CO["rows"] + 4200:CO["rows"] + 4201], 0.0))

        WQT = 8192 + 2176

        def wqr(name):
            offs = {"wt0": 0, "wt1": 2816, "zn": 5632, "az": 8448,
                    "acc": 11264, "sg": 14080, "am": 16896, "rs": 17072}
            sizes = {"wt0": 2816, "wt1": 2816, "zn": 2816, "az": 2816,
                     "acc": 2816, "sg": 2816, "am": 176, "rs": 176}
            return PF[:, WQT + offs[name]:WQT + offs[name] + sizes[name]]

        wq_scr = {k: wqr(k) for k in ("zn", "az", "acc", "sg", "am", "rs")}
        wq_tiles = []
        for key, hbm, ktot in (("g", wgate_hbm, D), ("u", wup_hbm, D),
                               ("d", wdown_hbm, FF)):
            nrows = (FF if key != "d" else D) // 128
            for nt in range(nrows):
                wq_tiles.append((key, hbm, nt, ktot))
        wq_state = {"prev": [None, None], "stage_prev": [None, None]}
        WQS = [24576, 24576 + 256]

        def emit_wq_tile(idx):
            key, hbm, nt, ktot = wq_tiles[idx]
            buf = wqr("wt0") if idx % 2 == 0 else wqr("wt1")
            buf = buf[:, 0:ktot]
            dtok = sch.dma([wq_state["prev"][idx % 2]],
                           lambda b: b.dma_start(buf, hbm[nt * 128:(nt + 1) * 128, :]),
                           slot="wq%d" % (idx % 2))
            qtok = emit_quant(buf, buf, ktot, wq_scr,
                              [dtok, wq_state["prev"][1 - idx % 2]])
            wq_state["prev"][idx % 2] = qtok
            park = wq_park[key]
            NTOT = FF if key != "d" else D
            last = None
            for kc in range(ktot // 128):
                so = WQS[kc % 2]
                hi = PB[:, so:so + 128]
                lo = PB[:, so + 128:so + 256]
                ltok = emit_tp(buf[:, kc * 128:(kc + 1) * 128], hi, lo,
                               [qtok, wq_state["stage_prev"][kc % 2]])
                sch.dma([ltok], lambda b, kc=kc, nt=nt, park=park, NTOT=NTOT,
                        hi=hi: b.dma_start(
                            bass.AP(park.tensor, kc * 128 * NTOT + nt * 128,
                                    [[NTOT, 128], [1, 128]]), hi),
                        slot="wqp%d" % (kc % 2))
                dpk = sch.dma([ltok], lambda b, kc=kc, nt=nt, park=park,
                              NTOT=NTOT, lo=lo: b.dma_start(
                                  bass.AP(park.tensor,
                                          D * FF + kc * 128 * NTOT + nt * 128,
                                          [[NTOT, 128], [1, 128]]), lo),
                              slot="wqp%d" % (kc % 2))
                wq_state["stage_prev"][kc % 2] = dpk
                last = dpk
            return last

        XB = [0, 1024]
        JUNK = 2048
        XN = 3072
        LNC = 4096
        QSB = [4112, 4112 + 1536]

        def lnc(i):
            return PF[:, LNC + i:LNC + i + 1]

        a1_state = {"xcons": [prologue_done, prologue_done],
                    "qcons": [None, None], "ncev": {}}
        wq_emitted = [0]
        wq_last = [None]

        def emit_a1_tile(i, half):
            XF = 26624
            if half == 1:
                ev = sch.dma([a1_state.get("xfguard"),
                              sch.dma_total("xnp0"), sch.dma_total("xnp1")],
                             lambda b, i=i: b.dma_start(
                                 PB[:, XF:XF + 2048],
                                 xn_park[i * 128:(i + 1) * 128, :]),
                             slot="xnl%d" % (i % 2))
                return emit_a1_mms(i, half, ev)
            dx = sch.dma([a1_state["xcons"][i % 2]], lambda b, i=i: b.dma_start(
                PF[:, XB[i % 2]:XB[i % 2] + 1024], x_hbm[i * 128:(i + 1) * 128, :]),
                slot="xb%d" % (i % 2))
            xb = PF[:, XB[i % 2]:XB[i % 2] + 1024]
            junk = PF[:, JUNK:JUNK + 1024]
            xn = PF[:, XN:XN + 1024]
            t1 = sch.add("scalar", [dx], lambda s: s.activation(
                junk, xb, AF.Copy, accum_out=lnc(0)))
            t2 = sch.add("scalar", [t1], lambda s: s.activation(
                lnc(1), lnc(0), AF.Copy, scale=-1.0 / D))
            t3 = sch.add("scalar", [t2], lambda s: s.activation(
                junk, xb, AF.Square, bias=lnc(1), accum_out=lnc(2)))
            t4 = sch.add("scalar", [t3], lambda s: s.activation(
                lnc(3), lnc(2), AF.Copy, scale=1.0 / D, bias=1e-5))
            t5 = sch.add("scalar", [t4], lambda s: s.activation(
                lnc(4), lnc(3), AF.Ln))
            t6 = sch.add("scalar", [t5], lambda s: s.activation(
                lnc(5), lnc(4), AF.Exp, scale=-0.5))
            t7 = sch.add("vector", [t6], lambda v: v.tensor_scalar(
                lnc(6), lnc(1), lnc(5), None, OP.mult))
            t8 = sch.add("vector", [t7], lambda v: v.tensor_scalar(
                xn, xb, lnc(5), lnc(6), OP.mult, OP.add))
            a1_state["xcons"][i % 2] = t8
            if i == 0 and half == 0 and dbg_out("ln", (128, 8)):
                sch.dma([t8], lambda b: b.dma_start(
                    dbg["ln"], PF[:, LNC:LNC + 8]), slot="dbg1")
            if i == 0 and half == 0 and dbg_out("lnx", (128, 1024)):
                sch.dma([t8], lambda b: b.dma_start(dbg["lnx"], xn),
                        slot="dbg1")
            if _noln:
                return t8
            ev = None
            for kc in range(8):
                ev = emit_tp(xn[:, kc * 128:(kc + 1) * 128],
                             PB[:, XF + kc * 128:XF + kc * 128 + 128],
                             PB[:, XF + 1024 + kc * 128:XF + 1024 + kc * 128 + 128],
                             [t8, a1_state.get("xfguard"),
                              a1_state.get("xfpark")])
            if _nomm:
                return ev
            a1_state["xfpark"] = sch.dma([ev], lambda b, i=i: b.dma_start(
                xn_park[i * 128:(i + 1) * 128, :], PB[:, XF:XF + 2048]),
                slot="xnp%d" % (i % 2))
            return emit_a1_mms(i, half, ev)

        def emit_a1_mms(i, half, ev):
            XF = 26624
            _onemm = bool(_os.environ.get("K_ONE_MM"))
            _nnch = int(_os.environ.get("K_NCH", "3"))
            qdep = a1_state["qcons"][i % 2]
            _bof = int(_os.environ.get("K_BOFF", "1"))
            for nch in range(1 if _onemm else _nnch):
                bank = (nch % 2) * _bof
                gch = half * 3 + nch

                def mmall(t, nch=nch, bank=bank):
                    last = None
                    for kc in range(8):
                        xh = PB[:, XF + kc * 128:XF + kc * 128 + 128]
                        xl = PB[:, XF + 1024 + kc * 128:XF + 1024 + kc * 128 + 128]
                        wh = PB[:, kc * 1536 + nch * 512:kc * 1536 + nch * 512 + 512]
                        wl = PB[:, 12288 + kc * 1536 + nch * 512:
                                12288 + kc * 1536 + nch * 512 + 512]
                        t.matmul(pb[bank], xh, wh, start=(kc == 0), stop=False)
                        t.matmul(pb[bank], xh, wl, start=False, stop=False)
                        last = t.matmul(pb[bank], xl, wh, start=False, stop=False)
                    return last
                import os as _os2
                _ser = bool(_os2.environ.get("K_SER_EVAC"))
                ob = _bof - bank if bank else _bof
                mtok = pe_w(bank, [ev, qdep] +
                            ([a1_state["ncev"].get(ob)] if _ser else []),
                            mmall)

                def bmm(t, bank=bank, gch=gch):
                    if _nobias:
                        return t.matmul(pb[bank],
                                        PB[:, 0:128], PB[:, 0:512],
                                        start=False, stop=True)
                    return t.matmul(pb[bank], ones_row,
                                    biasp_row[:, gch * 512:(gch + 1) * 512],
                                    start=False, stop=True)
                btok = sch.add("tensor", [t_ones], bmm)
                bank_w[bank] = btok
                a1_state["xfguard"] = btok
                etok = rd_bank("scalar", bank, [],
                               lambda s, nch=nch, i=i, bank=bank:
                               s.activation(PF[:, QSB[i % 2] + nch * 512:
                                               QSB[i % 2] + (nch + 1) * 512],
                                            pb[bank], AF.Copy))
                a1_state["ncev"][bank] = etok
            dq = sch.dma([a1_state["ncev"].get(0),
                          a1_state["ncev"].get(_bof) or
                          a1_state["ncev"].get(0)],
                         lambda b, i=i, half=half: b.dma_start(
                             qkv_park[i * 128:(i + 1) * 128,
                                      half * 1536:(half + 1) * 1536],
                             PF[:, QSB[i % 2]:QSB[i % 2] + 1536]),
                         slot="qp%d" % (i % 2))
            a1_state["qcons"][i % 2] = dq
            return dq

        if _lvl == 1:
            wqkv_prep_half(0)
        import os as _os
        _nowq = bool(_os.environ.get("K_NO_WQ"))
        _notp = bool(_os.environ.get("K_NO_TILES"))
        _nomm = bool(_os.environ.get("K_NO_MM"))
        _nobias = bool(_os.environ.get("K_NO_BIAS"))
        _noln = bool(_os.environ.get("K_NO_LNTP"))
        last_a1 = None
        for half in (range(2) if _lvl >= 2 else ()):
            wp_tok = wqkv_prep_half(half)
            a1_state["qcons"] = [wp_tok, wp_tok]
            for i in range(ntiles if not _notp else 0):
                last_a1 = emit_a1_tile(i, half)
                if not _nowq and wq_emitted[0] < len(wq_tiles):
                    wq_last[0] = emit_wq_tile(wq_emitted[0])
                    wq_emitted[0] += 1
        while _lvl >= 2 and not _nowq and wq_emitted[0] < len(wq_tiles):
            wq_last[0] = emit_wq_tile(wq_emitted[0])
            wq_emitted[0] += 1

        if dbg_out("qkv", (TOK, QKVD)):
            sch.dma([last_a1, sch.dma_total("qp0"), sch.dma_total("qp1")],
                    lambda b: b.dma_start(dbg["qkv"], qkv_park),
                    slot="dbg2")
        if dbg_out("wgq", (2, D, FF), BF16):
            sch.dma([wq_last[0], sch.dma_total("wqp0"), sch.dma_total("wqp1")],
                    lambda b: b.dma_start(dbg["wgq"], wq_park["g"]), slot="dbg3")

        a15_prev = [last_a1, last_a1]
        a15_readers = []
        for nt in (range(8) if _lvl >= 3 else ()):
            dtok = sch.dma([a15_prev[nt % 2], wq_last[0]],
                           lambda b, nt=nt: b.dma_start(
                               PF[:, WLD[nt % 2]:WLD[nt % 2] + 1024],
                               wout_hbm[nt * 128:(nt + 1) * 128, :]),
                           slot="wo%d" % (nt % 2))
            ltok = None
            for kc in range(8):
                hoff = kc * 1024 + nt * 128
                ltok = emit_tp(PF[:, WLD[nt % 2] + kc * 128:
                                  WLD[nt % 2] + kc * 128 + 128],
                               PB[:, hoff:hoff + 128],
                               PB[:, 8192 + hoff:8192 + hoff + 128], [dtok])
            a15_prev[nt % 2] = ltok
            a15_readers.append(ltok)
        t_wo = sch.add("vector", a15_readers, lambda v: v.memset(consts[0:1, CO["rows"] + 4200:CO["rows"] + 4201], 0.0)) \
            if _lvl >= 3 else None

        A2 = dict(qsb0=0, qsb1=3072, sq=6144, xb2=6144 + 2048 + 1024,
                  qe=9216, ke=11264, vq=13312, pp0=15360, pp1=17152,
                  uw=18944, pch=19200, scrc=19456, scr2=19584, nrm=19840,
                  vsum=19968, m2c=20224, rinv=20352, coh=20480, P=20736,
                  s2=20864, ec=20928, den=20992, wmat=21008, wt=21072,
                  m2w=23120, rr6=24144, fac=25168, fin=26192)
        A2["gsc"] = A2["sq"]
        A2["xb2"] = A2["sq"] + 2048
        A2["m1s"] = A2["rr6"]
        A2["m2s"] = A2["m2w"]
        A2["outr"] = A2["fac"]

        def pa(name, n):
            return PF[:, A2[name]:A2[name] + n]

        def nrmc(i, n=1):
            return PF[:, A2["nrm"] + i:A2["nrm"] + i + n]

        XF2 = 16384

        a2_state = {"qcons": [t_wo, t_wo], "x2cons": None, "xb2cons": None,
                    "finguard": t_wo}
        a2_lnstash = []
        a2_parks = []

        def entangle_state(zkey, sqoff, start_deps):
            zbuf = pa(zkey, 2048)
            pp0 = pa("pp0", 1792)
            pp1 = pa("pp1", 1792)
            scrc7 = PF[:, A2["uw"]:A2["uw"] + 896]
            bufs = [zbuf, pp0, pp1, zbuf]
            uwbufs = [pp1, zbuf, pp0]
            tokv = None
            for r in range(3):
                src, dst = bufs[r], bufs[r + 1]
                uw7 = uwbufs[r]
                scr2x = dst[:, 0:1792]

                def s2_of(i, r=r):
                    if i == 6:
                        return zbuf[:, 7 * 256:8 * 256]
                    if r == 0:
                        return zbuf[:, (i + 1) * 256:(i + 2) * 256]
                    return src[:, (i + 1) * 256:(i + 2) * 256]

                if r == 0:
                    for i in range(7):
                        tctrl = sch.add("vector",
                                        start_deps if i == 0 else [],
                                        lambda v, i=i: v.tensor_scalar(
                                            scrc7[:, i * 128:(i + 1) * 128],
                                            PF[:, A2["sq"] + sqoff + i * 128:
                                               A2["sq"] + sqoff + i * 128 + 128],
                                            nrmc(40 + (sqoff // 1024) * 8 + i),
                                            None, OP.mult))
                else:
                    tsqr = sch.add("scalar", [tokv], lambda s, src=src,
                                   scr2x=scr2x: s.activation(
                                       scr2x, src[:, 0:1792], AF.Square))
                    tctrl = sch.add("vector", [tsqr], lambda v, scr2x=scr2x:
                                    v.tensor_tensor(
                                        scrc7.rearrange("p (i d) -> p i d", i=7),
                                        scr2x.rearrange("p (i c d) -> p i c d",
                                                        i=7, c=2)[:, :, 0, :],
                                        scr2x.rearrange("p (i c d) -> p i c d",
                                                        i=7, c=2)[:, :, 1, :],
                                        OP.add))
                sc_toks = []
                for i in range(7):
                    ci = r * 7 + i
                    s2 = s2_of(i)
                    dep = (list(start_deps) + [tokv]) if i == 0 else []
                    sc_toks.append(sch.add("scalar", dep,
                                   lambda s, s2=s2, ci=ci, i=i, uw7=uw7:
                                   s.activation(uw7[:, i * 256:i * 256 + 128],
                                                s2[:, 128:256], AF.Copy,
                                                scale=col("B", 1, ci))))
                    sc_toks.append(sch.add("scalar", [],
                                   lambda s, s2=s2, ci=ci, i=i, uw7=uw7:
                                   s.activation(uw7[:, i * 256 + 128:
                                                    (i + 1) * 256],
                                                s2[:, 0:128], AF.Copy,
                                                scale=col("B", 1, ci))))
                vt_toks = []
                for i in range(7):
                    ci = r * 7 + i
                    s2 = s2_of(i)
                    vt_toks.append(sch.add("vector", [sc_toks[2 * i]],
                                   lambda v, s2=s2, ci=ci, i=i, uw7=uw7:
                                   v.scalar_tensor_tensor(
                                       uw7[:, i * 256:i * 256 + 128],
                                       s2[:, 0:128], col("A", 1, ci),
                                       uw7[:, i * 256:i * 256 + 128],
                                       OP.mult, OP.add)))
                    vt_toks.append(sch.add("vector", [sc_toks[2 * i + 1]],
                                   lambda v, s2=s2, ci=ci, i=i, uw7=uw7:
                                   v.scalar_tensor_tensor(
                                       uw7[:, i * 256 + 128:(i + 1) * 256],
                                       s2[:, 128:256], col("C", 1, ci),
                                       uw7[:, i * 256 + 128:(i + 1) * 256],
                                       OP.mult, OP.add)))
                for i in range(7):
                    sch.add("gpsimd", [vt_toks[2 * i], vt_toks[2 * i + 1],
                                       tctrl],
                            lambda g, i=i, dst=dst, uw7=uw7:
                            g.tensor_tensor(
                                dst[:, i * 256:(i + 1) * 256].rearrange(
                                    "p (a b) -> p a b", a=2),
                                uw7[:, i * 256:(i + 1) * 256].rearrange(
                                    "p (a b) -> p a b", a=2),
                                scrc7[:, i * 128:(i + 1) * 128].unsqueeze(
                                    1).broadcast_to([128, 2, 128]),
                                OP.mult))
                for i in range(7):
                    s2 = s2_of(i)
                    tokv = sch.add("gpsimd", [], lambda g, i=i, dst=dst, s2=s2:
                                   g.tensor_tensor(dst[:, i * 256:(i + 1) * 256],
                                                   dst[:, i * 256:(i + 1) * 256],
                                                   s2, OP.add))
            return tokv

        def emit_a2_tile(i):
            qsb = pa("qsb0" if i % 2 == 0 else "qsb1", QKVD)
            dq = sch.dma([a2_state["qcons"][i % 2],
                          a2_state.get("qcons_ln"),
                          sch.dma_total("qp0"), sch.dma_total("qp1")] +
                         a2_state.get("dbgguard", []),
                         lambda b, i=i, qsb=qsb:
                         b.dma_start(qsb, qkv_park[i * 128:(i + 1) * 128, :]),
                         slot="qi%d" % (i % 2))
            sq = pa("sq", QKVD)
            t_sq = sch.add("scalar", [dq, a2_state.get("sqguard")],
                           lambda s: s.activation(sq, qsb, AF.Square))
            t_ss = sch.add("vector", [t_sq], lambda v: v.tensor_reduce(
                nrmc(0, 24), sq.rearrange("p (h d) -> p h d", h=24), AX.X, OP.add))
            t_mx = sch.add("vector", [t_ss], lambda v: v.tensor_scalar(
                nrmc(0, 24), nrmc(0, 24), 1e-24, None, OP.max))
            t_ln = sch.add("scalar", [t_mx], lambda s: s.activation(
                nrmc(0, 24), nrmc(0, 24), AF.Ln))
            t_rn = sch.add("scalar", [t_ln], lambda s: s.activation(
                nrmc(0, 24), nrmc(0, 24), AF.Exp, scale=-0.5))
            t_rn2 = sch.add("vector", [t_rn], lambda v: v.tensor_tensor(
                nrmc(40, 16), nrmc(0, 16), nrmc(0, 16), OP.mult))
            enc_toks = []
            for z, key in ((0, "qe"), (1, "ke"), (2, "vq")):
                for h in range(8):
                    src = qsb[:, (z * 8 + h) * 128:(z * 8 + h) * 128 + 128]
                    for ci, trig in ((0, "cos"), (1, "sin")):
                        def ef(v, src=src, key=key, h=h, ci=ci, trig=trig, z=z):
                            return v.scalar_tensor_tensor(
                                pa(key, 2048)[:, h * 256 + ci * 128:
                                              h * 256 + ci * 128 + 128],
                                src, nrmc(z * 8 + h), cv(trig, 128),
                                OP.mult, OP.mult)
                        enc_toks.append(sch.add("vector", [t_rn], ef))
            t_enc = enc_toks[-1]
            tq = entangle_state("qe", 0, [t_enc])
            tk = entangle_state("ke", 1024, [t_enc, tq])
            t_vs = sch.add("vector", [t_enc], lambda v: v.tensor_reduce(
                pa("vsum", 256).rearrange("p (c d) -> p c d", c=2),
                pa("vq", 2048).rearrange("p (j c d) -> p c d j", j=8, c=2),
                AX.X, OP.add))
            t_m2 = sch.add("scalar", [t_vs], lambda s: s.activation(
                pa("scr2", 256), pa("vsum", 256), AF.Square))
            t_m2b = sch.add("vector", [t_m2], lambda v: v.tensor_tensor(
                pa("m2c", 128), pa("scr2", 256)[:, 0:128],
                pa("scr2", 256)[:, 128:256], OP.add))
            t_m2c = sch.add("vector", [t_m2b], lambda v: v.tensor_scalar(
                pa("m2c", 128), pa("m2c", 128), 1e-30, None, OP.max))
            t_li = sch.add("scalar", [t_m2c], lambda s: s.activation(
                pa("rinv", 128), pa("m2c", 128), AF.Ln))
            t_ri = sch.add("scalar", [t_li], lambda s: s.activation(
                pa("rinv", 128), pa("rinv", 128), AF.Exp, scale=-0.5))
            t_coh = sch.add("vector", [t_ri], lambda v: v.tensor_tensor(
                pa("coh", 256).rearrange("p (c d) -> p c d", c=2),
                pa("vsum", 256).rearrange("p (c d) -> p c d", c=2),
                pa("rinv", 128).unsqueeze(1).broadcast_to([128, 2, 128]),
                OP.mult))
            gsc = pa("gsc", 2048)
            gsci = PF[:, A2["m2w"]:A2["m2w"] + 2048]
            qe4 = pa("qe", 2048).rearrange("p (i c d) -> p i c d", i=8, c=2)
            gv = gsc.rearrange("p (i c d) -> p i c d", i=8, c=2)
            gvi = gsci.rearrange("p (i c d) -> p i c d", i=8, c=2)
            t_gram = [tq, tk]
            g3 = g6 = None
            for j in range(8):
                kr = pa("ke", 2048)[:, j * 256:j * 256 + 128]
                ki = pa("ke", 2048)[:, j * 256 + 128:j * 256 + 256]
                krb = kr.unsqueeze(1).broadcast_to([128, 8, 128])
                kib = ki.unsqueeze(1).broadcast_to([128, 8, 128])
                g1 = sch.add("vector", t_gram if j == 0 else [], lambda v, krb=krb:
                             v.tensor_tensor(gv[:, :, 0, :], qe4[:, :, 0, :],
                                             krb, OP.mult))
                g2 = sch.add("vector", [], lambda v, kib=kib: v.tensor_tensor(
                    gv[:, :, 1, :], qe4[:, :, 1, :], kib, OP.mult))
                g3 = sch.add("vector", [], lambda v, j=j: v.tensor_reduce(
                    pa("P", 128)[:, j * 8:j * 8 + 8],
                    gsc.rearrange("p (i e) -> p i e", i=8), AX.X, OP.add))
                g4 = sch.add("gpsimd", (t_gram + [a2_state.get("grguard")])
                             if j == 0 else [g6], lambda g, kib=kib:
                             g.tensor_tensor(gvi[:, :, 0, :], qe4[:, :, 0, :],
                                             kib, OP.mult))
                g5 = sch.add("gpsimd", [], lambda g, krb=krb:
                             g.tensor_tensor(gvi[:, :, 1, :], qe4[:, :, 1, :],
                                             krb, OP.mult))
                g6a = sch.add("vector", [g5], lambda v: v.tensor_reduce(
                    pa("scr2", 256)[:, 0:16].rearrange("p (i c) -> p i c", c=2),
                    gsci.rearrange("p (i c d) -> p i c d", i=8, c=2),
                    AX.X, OP.add))
                g6 = sch.add("vector", [g6a], lambda v, j=j: v.tensor_tensor(
                    pa("P", 128)[:, 64 + j * 8:64 + j * 8 + 8],
                    pa("scr2", 256)[:, 0:16].rearrange(
                        "p (i c) -> p i c", c=2)[:, :, 0],
                    pa("scr2", 256)[:, 0:16].rearrange(
                        "p (i c) -> p i c", c=2)[:, :, 1],
                    OP.subtract))
            t_P = sch.add("vector", [g3, g6], lambda v: v.memset(consts[0:1, CO["rows"] + 4200:CO["rows"] + 4201], 0.0))
            t_s2a = sch.add("scalar", [t_P], lambda s: s.activation(
                pa("s2", 64), pa("P", 128)[:, 0:64], AF.Square))
            t_s2b = sch.add("scalar", [t_s2a], lambda s: s.activation(
                pa("ec", 64), pa("P", 128)[:, 64:128], AF.Square))
            t_s2 = sch.add("vector", [t_s2b], lambda v: v.tensor_tensor(
                pa("s2", 64), pa("s2", 64), pa("ec", 64), OP.add))
            t_fz = sch.add("gpsimd", [a2_state.get("finguard")] +
                           a2_state.pop("dbgguard", []),
                           lambda g: g.memset(pa("fin", 2048), 0.0))
            t_fin = t_fz
            ECB = A2["pp1"] + 256
            WTP = [A2["wt"], A2["pp0"]]

            def pec(c):
                return PF[:, ECB + c * 64:ECB + (c + 1) * 64]

            def pden(c):
                return PF[:, ECB + 256 + c * 16:ECB + 256 + (c + 1) * 16]

            def pwm(c):
                return PF[:, ECB + 320 + c * 64:ECB + 320 + (c + 1) * 64]

            def pwt(c):
                return PF[:, WTP[c % 2]:WTP[c % 2] + 2048]

            sm_toks = {}

            def emit_sm(cix):
                t_e = sch.add("scalar", [t_s2, t_bcgam], lambda s, cix=cix:
                              s.activation(pec(cix), pa("s2", 64), AF.Exp,
                                           scale=col("gam", 1, cix)))
                t_dn = sch.add("vector", [t_e], lambda v, cix=cix:
                               v.tensor_reduce(
                    pden(cix)[:, 0:8],
                    pec(cix).rearrange("p (j i) -> p i j", j=8), AX.X, OP.add))
                t_rd = sch.add("vector", [t_dn], lambda v, cix=cix:
                               v.reciprocal(pden(cix)[:, 8:16],
                                            pden(cix)[:, 0:8]))
                t_pr = sch.add("vector", [t_rd], lambda v, cix=cix:
                               v.tensor_tensor(
                    pwm(cix).rearrange("p (j i) -> p j i", j=8),
                    pec(cix).rearrange("p (j i) -> p j i", j=8),
                    pden(cix)[:, 8:16].unsqueeze(1).broadcast_to([128, 8, 8]),
                    OP.mult))
                t_pr2 = sch.add("vector", [t_pr], lambda v, cix=cix:
                                v.tensor_scalar(pwm(cix), pwm(cix), 1e-12,
                                                None, OP.add))
                t_lnp = sch.add("scalar", [t_pr2], lambda s, cix=cix:
                                s.activation(pwm(cix), pwm(cix), AF.Ln))
                t_w = sch.add("scalar", [t_lnp], lambda s, cix=cix:
                              s.activation(pwm(cix), pwm(cix), AF.Exp,
                                           scale=0.5))
                ap_toks = []
                for ii in range(8):
                    tok = sch.add("vector", [t_w], lambda v, ii=ii, cix=cix:
                                  v.tensor_scalar(
                                      pwt(cix)[:, ii * 256:(ii + 1) * 256],
                                      pa("vq", 2048)[:, 0:256],
                                      pwm(cix)[:, ii:ii + 1],
                                      None, OP.mult))
                    for j in range(1, 8):
                        tok = sch.add("vector", [tok],
                                      lambda v, ii=ii, j=j, cix=cix:
                                      v.scalar_tensor_tensor(
                                          pwt(cix)[:, ii * 256:(ii + 1) * 256],
                                          pa("vq", 2048)[:, j * 256:(j + 1) * 256],
                                          pwm(cix)[:, j * 8 + ii:j * 8 + ii + 1],
                                          pwt(cix)[:, ii * 256:(ii + 1) * 256],
                                          OP.mult, OP.add))
                    ap_toks.append(tok)
                sm_toks[cix] = (t_e, t_rd, t_w, ap_toks)

            def emit_st6(cix, t_fin, gsc_guard):
                t_ap = sm_toks[cix][3]
                t_q2 = sch.add("scalar", t_ap + [gsc_guard],
                               lambda s, cix=cix: s.activation(
                                   pa("gsc", 2048), pwt(cix), AF.Square))
                wt4 = pa("gsc", 2048).rearrange("p (i c d) -> p i c d", i=8, c=2)
                t_m2w = sch.add("vector", [t_q2], lambda v: v.tensor_tensor(
                    pa("m2w", 1024).rearrange("p (i d) -> p i d", i=8),
                    wt4[:, :, 0, :], wt4[:, :, 1, :], OP.add))
                t_m2x = sch.add("vector", [t_m2w], lambda v: v.tensor_scalar(
                    pa("m2w", 1024), pa("m2w", 1024), 1e-30, None, OP.max))
                t_l6 = sch.add("scalar", [t_m2x], lambda s: s.activation(
                    pa("rr6", 1024), pa("m2w", 1024), AF.Ln))
                t_r6 = sch.add("scalar", [t_l6], lambda s: s.activation(
                    pa("rr6", 1024), pa("rr6", 1024), AF.Exp, scale=-0.5))
                t_f1 = sch.add("vector", [t_r6], lambda v: v.tensor_scalar(
                    pa("fac", 1024), pa("rr6", 1024), 1.0, None, OP.min))
                t_f2 = sch.add("vector", [t_f1], lambda v: v.scalar_tensor_tensor(
                    pa("fac", 1024), pa("rr6", 1024), 1e-6, pa("fac", 1024),
                    OP.mult, OP.max))
                t_fs = sch.add("vector", [t_f2, t_bcsup], lambda v, cix=cix:
                               v.tensor_scalar(pa("fac", 1024), pa("fac", 1024),
                                               col("supw", 1, cix), None,
                                               OP.mult))
                fs4 = pa("fac", 1024).rearrange("p (i d) -> p i d", i=8
                                                ).unsqueeze(2).broadcast_to(
                                                    [128, 8, 2, 128])
                t_tm = sch.add("vector", [t_fs], lambda v, cix=cix:
                               v.tensor_tensor(
                    pa("gsc", 2048).rearrange("p (i c d) -> p i c d", i=8, c=2),
                    pwt(cix).rearrange("p (i c d) -> p i c d", i=8, c=2),
                    fs4, OP.mult))
                t_fin = sch.add("vector", [t_tm, t_fin] + findg[:], lambda v:
                                v.tensor_tensor(pa("fin", 2048),
                                                pa("fin", 2048),
                                                pa("gsc", 2048), OP.add))
                del findg[:]
                if i == 0 and dbg_out("finc%d" % cix, (128, 2048)):
                    dtok = sch.dma([t_fin], lambda b, cix=cix: b.dma_start(
                        dbg["finc%d" % cix], pa("fin", 2048)), slot="dbg1")
                    findg.append(dtok)
                    a2_state.setdefault("dbgguard", []).append(dtok)
                return t_fin, t_fin

            emit_sm(0)
            gsc_guard = None
            findg = []
            for cix in range(4):
                if cix < 3:
                    emit_sm(cix + 1)
                t_fin, gsc_guard = emit_st6(cix, t_fin, gsc_guard)
            if i == 0:
                for nm, src, dtok in (("c0ec", pec(0), sm_toks[0][0]),
                                      ("c0den", pden(0), sm_toks[0][1]),
                                      ("c0w", pwm(0), sm_toks[0][2]),
                                      ("cgam", col("gam", 4), t_bcgam),
                                      ("csup", col("supw", 4), t_bcsup)):
                    if dbg_out(nm, (128, src.shape[-1])):
                        a2_state.setdefault("dbgguard", []).append(
                            sch.dma([dtok], lambda b, nm=nm, src=src:
                                    b.dma_start(dbg[nm], src), slot="dbg1"))
            fin4 = pa("fin", 2048).rearrange("p (i c d) -> p i c d", i=8, c=2)
            cr = pa("coh", 256)[:, 0:128].unsqueeze(1).broadcast_to([128, 8, 128])
            cib = pa("coh", 256)[:, 128:256].unsqueeze(1).broadcast_to([128, 8, 128])
            t_o1 = sch.add("vector", [t_fin, t_coh], lambda v: v.tensor_tensor(
                pa("m1s", 1024).rearrange("p (i d) -> p i d", i=8),
                fin4[:, :, 0, :], cr, OP.mult))
            t_o2 = sch.add("vector", [t_o1], lambda v: v.tensor_tensor(
                pa("m2s", 1024).rearrange("p (i d) -> p i d", i=8),
                fin4[:, :, 1, :], cib, OP.mult))
            t_out = sch.add("vector", [t_o2], lambda v: v.tensor_tensor(
                pa("outr", 1024), pa("m1s", 1024), pa("m2s", 1024), OP.subtract))
            if i == 0 and dbg_out("a2rn", (128, 64)):
                a2_state.setdefault("dbgguard", []).append(
                    sch.dma([t_rn2], lambda b: b.dma_start(
                        dbg["a2rn"], PF[:, A2["nrm"]:A2["nrm"] + 64]),
                        slot="dbg1"))
            if i == 0 and dbg_out("trig", (128, 256)):
                a2_state.setdefault("dbgguard", []).append(
                    sch.dma([t_enc], lambda b: b.dma_start(
                        dbg["trig"], consts[:, CO["cos"]:CO["cos"] + 256]),
                        slot="dbg1"))
            if i == 0:
                for nm, src, dtok in (("a2qe", pa("qe", 2048), tq),
                                      ("a2ke", pa("ke", 2048), tk),
                                      ("a2vq", pa("vq", 2048), t_enc),
                                      ("a2coh", pa("coh", 256), t_coh),
                                      ("a2P", pa("P", 128), t_P),
                                      ("a2fin", pa("fin", 2048), t_fin),
                                      ("a2out", pa("outr", 1024), t_out)):
                    if dbg_out(nm, (128, src.shape[-1])):
                        a2_state.setdefault("dbgguard", []).append(
                            sch.dma([dtok], lambda b, nm=nm, src=src:
                                    b.dma_start(dbg[nm], src), slot="dbg1"))
            ev = None
            for kc in range(8):
                ev = emit_tp(pa("outr", 1024)[:, kc * 128:(kc + 1) * 128],
                             PB[:, XF2 + kc * 128:XF2 + kc * 128 + 128],
                             PB[:, XF2 + 1024 + kc * 128:XF2 + 1024 + kc * 128 + 128],
                             [t_out, a2_state.get("xf2guard")])
            dx2 = sch.dma([a2_state["xb2cons"], t_ss], lambda b, i=i: b.dma_start(
                pa("xb2", 1024), x_hbm[i * 128:(i + 1) * 128, :]), slot="ax2")
            x2parts = []
            for nch in range(2):
                bank = nch

                def mmo(t, nch=nch, bank=bank):
                    last = None
                    for kc in range(8):
                        xh = PB[:, XF2 + kc * 128:XF2 + kc * 128 + 128]
                        xl = PB[:, XF2 + 1024 + kc * 128:XF2 + 1024 + kc * 128 + 128]
                        wh = PB[:, kc * 1024 + nch * 512:kc * 1024 + nch * 512 + 512]
                        wl = PB[:, 8192 + kc * 1024 + nch * 512:
                                8192 + kc * 1024 + nch * 512 + 512]
                        t.matmul(pb[bank], xh, wh, start=(kc == 0), stop=False)
                        t.matmul(pb[bank], xh, wl, start=False, stop=False)
                        last = t.matmul(pb[bank], xl, wh, start=False, stop=False)
                    return last
                mtok = pe_w(bank, [ev, t_wo], mmo)
                btok = sch.add("tensor", [d_bo, t_ones], lambda t, nch=nch, bank=bank:
                               t.matmul(pb[bank], ones_row,
                                        bout_row[:, nch * 512:(nch + 1) * 512],
                                        start=False, stop=True))
                bank_w[bank] = btok
                a2_state["xf2guard"] = btok
                tga = rd_bank("vector", bank, [dx2], lambda v, nch=nch, bank=bank:
                              v.tensor_tensor(pa("m1s", 1024)[:, nch * 512:
                                                              (nch + 1) * 512],
                                              pb[bank],
                                              cv("ga", 1024)[:, nch * 512:
                                                             (nch + 1) * 512],
                                              OP.mult))
                tx2 = sch.add("vector", [tga], lambda v, nch=nch, i=i:
                              v.tensor_tensor(
                                  pa("qsb0" if i % 2 == 0 else "qsb1", QKVD)
                                  [:, nch * 512:(nch + 1) * 512],
                                  pa("m1s", 1024)[:, nch * 512:(nch + 1) * 512],
                                  pa("xb2", 1024)[:, nch * 512:(nch + 1) * 512],
                                  OP.add))
                x2parts.append(tx2)
            a2_state["xb2cons"] = x2parts[-1]
            a2_state["sqguard"] = x2parts[-1]
            a2_state["finguard"] = t_out
            a2_state["grguard"] = t_out
            x2v = pa("qsb0" if i % 2 == 0 else "qsb1", QKVD)[:, 0:1024]
            jnk = pa("m2w", 1024)
            l1 = sch.add("scalar", x2parts, lambda s: s.activation(
                jnk, x2v, AF.Copy, accum_out=nrmc(60)))
            l2 = sch.add("scalar", [l1], lambda s: s.activation(
                nrmc(61), nrmc(60), AF.Copy, scale=-1.0 / D))
            l3 = sch.add("scalar", [l2], lambda s: s.activation(
                jnk, x2v, AF.Square, bias=nrmc(61), accum_out=nrmc(62)))
            l4 = sch.add("scalar", [l3], lambda s: s.activation(
                nrmc(63), nrmc(62), AF.Copy, scale=1.0 / D, bias=1e-5))
            l5 = sch.add("scalar", [l4], lambda s: s.activation(
                nrmc(63), nrmc(63), AF.Ln))
            l6 = sch.add("scalar", [l5], lambda s, i=i: s.activation(
                cv("b_ln", 128)[:, 2 * i:2 * i + 1], nrmc(63), AF.Exp,
                scale=-0.5))
            l7 = sch.add("vector", [l6], lambda v, i=i: v.tensor_scalar(
                cv("b_ln", 128)[:, 2 * i + 1:2 * i + 2], nrmc(61),
                cv("b_ln", 128)[:, 2 * i:2 * i + 1], None, OP.mult))
            a2_lnstash.append(l7)
            dx2p = sch.dma(x2parts + [l3], lambda b, i=i: b.dma_start(
                x2_park[i * 128:(i + 1) * 128, :],
                pa("qsb0" if i % 2 == 0 else "qsb1", QKVD)[:, 0:1024]),
                slot="xp%d" % (i % 2))
            a2_parks.append(dx2p)
            a2_state["qcons"][i % 2] = dx2p
            a2_state["qcons_ln"] = l3
            return dx2p

        last_a2 = None
        for i in (range(ntiles) if _lvl >= 4 else ()):
            last_a2 = emit_a2_tile(i)

        if dbg_out("x2", (TOK, D)):
            sch.dma([last_a2, sch.dma_total("xp0"), sch.dma_total("xp1")],
                    lambda b: b.dma_start(dbg["x2"], x2_park), slot="dbg4")

        B = dict(acc=0, x2b=4096, h2=5120, lnc=6144, junk=6160,
                 qzn=7184, qaz=8208, qacc=9232, qsg=10256, qam=11280, qrs=11344,
                 sgs=11504, hch=11760, z2=12016, a2_=12272, c2=12528, s2_=12784,
                 m2=13040, r2=13056, yt=13072)
        bq_scr = {"zn": PF[:, B["qzn"]:B["qzn"] + 1024],
                  "az": PF[:, B["qaz"]:B["qaz"] + 1024],
                  "acc": PF[:, B["qacc"]:B["qacc"] + 1024],
                  "sg": PF[:, B["qsg"]:B["qsg"] + 1024],
                  "am": PF[:, B["qam"]:B["qam"] + 64],
                  "rs": PF[:, B["qrs"]:B["qrs"] + 64]}
        bh_scr = {"zn": PF[:, B["z2"]:B["z2"] + 256],
                  "az": PF[:, B["a2_"]:B["a2_"] + 256],
                  "acc": PF[:, B["c2"]:B["c2"] + 256],
                  "sg": PF[:, B["s2_"]:B["s2_"] + 256],
                  "am": PF[:, B["m2"]:B["m2"] + 16],
                  "rs": PF[:, B["r2"]:B["r2"] + 16]}

        def pbv(off, n):
            return PB[:, off:off + n]

        def blnc(i_):
            return PF[:, B["lnc"] + i_:B["lnc"] + i_ + 1]

        B2 = dict(q4zn=14336, q4az=15360, q4acc=16384, q4sg=17408,
                  q4am=18432, q4rs=18496, hch4=18560, sgs2=19584,
                  hch4b=20096)
        bh4_scr = {"zn": PF[:, B2["q4zn"]:B2["q4zn"] + 1024],
                   "az": PF[:, B2["q4az"]:B2["q4az"] + 1024],
                   "acc": PF[:, B2["q4acc"]:B2["q4acc"] + 1024],
                   "sg": PF[:, B2["q4sg"]:B2["q4sg"] + 1024],
                   "am": PF[:, B2["q4am"]:B2["q4am"] + 64],
                   "rs": PF[:, B2["q4rs"]:B2["q4rs"] + 64]}
        b_state = {"x2c": last_a2, "wc": last_a2, "hq4": {},
                   "sgsg": [None, None], "hchg2": [[], []]}
        NCH = FF // 256
        last_y = None
        if _lvl >= 5:
            t_preb = sch.add("scalar", [], lambda s: s.preload_act_table(AF.Silu))
            for _ in range(3):
                t_preb = sch.add("scalar", [t_preb], lambda s: s.activation(
                    dummy_row, dummy_row, AF.Silu))
        for g in (range(NG) if _lvl >= 5 else ()):
            t_az = sch.add("gpsimd", [b_state["x2c"]], lambda g_: g_.memset(
                PF[:, B["acc"]:B["acc"] + 4096], 0.0))
            headtoks = []
            for j in range(4):
                ti = g * 4 + j
                dxt = sch.dma([b_state["x2c"],
                               a2_parks[ti] if _lvl >= 4 else None],
                              lambda b, ti=ti:
                              b.dma_start(PF[:, B["x2b"]:B["x2b"] + 1024],
                                          x2_park[ti * 128:(ti + 1) * 128, :]),
                              slot="bx")
                xb = PF[:, B["x2b"]:B["x2b"] + 1024]
                junk = PF[:, B["junk"]:B["junk"] + 1024]
                h2 = PF[:, B["h2"]:B["h2"] + 1024]
                s8 = sch.add("vector", [dxt, a2_lnstash[ti] if _lvl >= 4
                                        else None],
                             lambda v, ti=ti: v.tensor_scalar(
                                 junk, xb, cv("b_ln", 128)[:, 2 * ti:2 * ti + 1],
                                 cv("b_ln", 128)[:, 2 * ti + 1:2 * ti + 2],
                                 OP.mult, OP.add))
                s9 = sch.add("vector", [s8], lambda v: v.tensor_tensor(
                    h2, junk, cv("cb1", 1024), OP.mult))
                sA = sch.add("vector", [s9], lambda v: v.tensor_tensor(
                    h2, h2, cv("sb_", 1024), OP.add))
                b_state["x2c"] = sA
                qt = emit_quant(h2, h2, 1024, bq_scr, [sA])
                ev = None
                for kc in range(8):
                    ho = j * 1024 + kc * 128
                    ev = emit_tp(h2[:, kc * 128:(kc + 1) * 128],
                                 pbv(ho, 128), pbv(4096 + ho, 128), [qt])
                headtoks.append(ev)
            t_head = sch.add("vector", headtoks, lambda v: v.memset(consts[0:1, CO["rows"] + 4200:CO["rows"] + 4201], 0.0))
            wq_done = [sch.dma_total("wqp0"), sch.dma_total("wqp1")] \
                if _lvl >= 2 else []

            def wload(b, ncix, key, dsto, hl):
                park = wq_park[key]
                NTOT = FF if key != "d" else D
                base = hl * D * FF + ncix * 256
                return b.dma_start(
                    pbv(dsto + hl * 2048, 2048).rearrange(
                        "p (kc e) -> p kc e", kc=8),
                    bass.AP(park.tensor, base,
                            [[NTOT, 128], [128 * NTOT, 8], [1, 256]]))

            def emit_dloads(ncix, deps):
                toks = []
                for hl in (0, 1):
                    def wdl(b, ncix=ncix, hl=hl):
                        base = hl * FF * D + ncix * 256 * D
                        return b.dma_start(
                            pbv(16384 + hl * 2048, 2048).rearrange(
                                "p (kq n) -> p kq n", kq=2),
                            bass.AP(wq_park["d"].tensor, base,
                                    [[D, 128], [128 * D, 2], [1, 1024]]))
                    toks.append(sch.dma(deps + wq_done, wdl,
                                        slot="bw%d" % (4 + hl)))
                return toks

            def emit_gu(ncix, first, t_head):
                wr_prev = b_state.get("wr_gu") or [b_state["wc"]]
                dws = []
                for si, (key, dsto) in enumerate((("g", 8192), ("u", 12288))):
                    for hl in (0, 1):
                        dws.append(sch.dma(
                            wr_prev + wq_done,
                            lambda b, key=key, dsto=dsto, hl=hl, ncix=ncix:
                            wload(b, ncix, key, dsto, hl),
                            slot="bw%d" % (si * 2 + hl)))
                if first:
                    b_state["dw_d"] = emit_dloads(
                        ncix, b_state.get("dread") or [b_state["wc"]])
                hch = PF[:, B2["hch4" if ncix % 2 == 0 else "hch4b"]:
                         B2["hch4" if ncix % 2 == 0 else "hch4b"] + 1024]
                th_toks = []
                mgmu = []
                for j in range(4):
                    def mmgu(t, j=j, base=8192):
                        last = None
                        for kc in range(8):
                            xh = pbv(j * 1024 + kc * 128, 128)
                            xl = pbv(4096 + j * 1024 + kc * 128, 128)
                            wh = pbv(base + kc * 256, 256)
                            wl = pbv(base + 2048 + kc * 256, 256)
                            t.matmul(pb[0 if base == 8192 else 1][:, 0:256],
                                     xh, wh, start=(kc == 0), stop=False)
                            t.matmul(pb[0 if base == 8192 else 1][:, 0:256],
                                     xh, wl, start=False, stop=False)
                            last = t.matmul(pb[0 if base == 8192 else 1][:, 0:256],
                                            xl, wh, start=False,
                                            stop=(kc == 7))
                        return last
                    mg = pe_w(0, [t_head] + dws, lambda t, j=j: mmgu(t, j, 8192))
                    mu = pe_w(1, [t_head] + dws, lambda t, j=j: mmgu(t, j, 12288))
                    sgs_j = PF[:, B2["sgs2"] + (j % 2) * 256:
                               B2["sgs2"] + (j % 2) * 256 + 256]
                    tsg = rd_bank("scalar", 0, [b_state["sgsg"][j % 2]],
                                  lambda s, sgs_j=sgs_j: s.activation(
                                      sgs_j, pb[0][:, 0:256], AF.Silu))
                    th = rd_bank("vector", 1,
                                 [tsg] + b_state["hchg2"][ncix % 2],
                                 lambda v, j=j, sgs_j=sgs_j, hch=hch:
                                 v.tensor_tensor(hch[:, j * 256:(j + 1) * 256],
                                                 sgs_j, pb[1][:, 0:256],
                                                 OP.mult))
                    b_state["sgsg"][j % 2] = th
                    th_toks.append(th)
                    mgmu.append(mg)
                    mgmu.append(mu)
                b_state["wr_gu"] = mgmu
                qh = emit_quant(hch, hch, 1024, bh4_scr, th_toks)
                return dict(qh=qh, hch=hch, ncix=ncix)

            def emit_tail(st, next_ncix, t_az):
                ncix, qh, hch = st["ncix"], st["qh"], st["hch"]
                hev = {}
                for j in range(4):
                    for kq in range(2):
                        lt = emit_tp(hch[:, j * 256 + kq * 128:
                                          j * 256 + kq * 128 + 128],
                                     pbv(20480 + j * 512 + kq * 128, 128),
                                     pbv(22528 + j * 512 + kq * 128, 128),
                                     [qh] + b_state["hq4"].get(j, []))
                        hev.setdefault(j, []).append(lt)
                b_state["hchg2"][ncix % 2] = [hev[j][1] for j in range(4)]
                dtoks = b_state["dw_d"]
                mds = []
                last_tacc = None
                for j in range(4):
                    for half in range(2):
                        bank = 2 + half

                        def mmd(t, j=j, half=half, bank=bank):
                            last = None
                            for kq in range(2):
                                xh = pbv(20480 + j * 512 + kq * 128, 128)
                                xl = pbv(22528 + j * 512 + kq * 128, 128)
                                wh = pbv(16384 + kq * 1024 + half * 512, 512)
                                wl = pbv(18432 + kq * 1024 + half * 512, 512)
                                t.matmul(pb[bank], xh, wh, start=(kq == 0),
                                         stop=False)
                                t.matmul(pb[bank], xh, wl, start=False,
                                         stop=False)
                                last = t.matmul(pb[bank], xl, wh, start=False,
                                                stop=(kq == 1))
                            return last
                        md = pe_w(bank, hev[j] + dtoks, mmd)
                        last_tacc = rd_bank(
                            "vector", bank, [t_az],
                            lambda v, j=j, half=half, bank=bank:
                            v.scalar_tensor_tensor(
                                PF[:, B["acc"] + j * 1024 + half * 512:
                                   B["acc"] + j * 1024 + half * 512 + 512],
                                pb[bank], 1.0,
                                PF[:, B["acc"] + j * 1024 + half * 512:
                                   B["acc"] + j * 1024 + half * 512 + 512],
                                OP.mult, OP.add))
                        mds.append(md)
                    b_state["hq4"][j] = [md]
                b_state["dread"] = mds
                b_state["wc"] = last_tacc
                if next_ncix is not None:
                    b_state["dw_d"] = emit_dloads(next_ncix, mds)

            st = emit_gu(0, True, t_head)
            for c in range(NCH):
                nst = emit_gu(c + 1, False, t_head) if c + 1 < NCH else None
                emit_tail(st, c + 1 if c + 1 < NCH else None, t_az)
                st = nst
            for j in range(4):
                ti = g * 4 + j
                dxt = sch.dma([b_state["x2c"],
                               a2_parks[ti] if _lvl >= 4 else None],
                              lambda b, ti=ti: b.dma_start(
                                  PF[:, B["x2b"]:B["x2b"] + 1024],
                                  x2_park[ti * 128:(ti + 1) * 128, :]),
                              slot="bx")
                tm = sch.add("vector", [dxt, b_state["wc"]], lambda v, j=j:
                             v.tensor_tensor(PF[:, B["junk"]:B["junk"] + 1024],
                                             PF[:, B["acc"] + j * 1024:
                                                B["acc"] + (j + 1) * 1024],
                                             cv("gb", 1024), OP.mult))
                ty = sch.add("vector", [tm], lambda v: v.tensor_tensor(
                    PF[:, B["yt"]:B["yt"] + 1024],
                    PF[:, B["junk"]:B["junk"] + 1024],
                    PF[:, B["x2b"]:B["x2b"] + 1024], OP.add))
                b_state["x2c"] = ty
                last_y = sch.dma([ty, last_y], lambda b, ti=ti: b.dma_start(
                    y_hbm[ti * 128:(ti + 1) * 128, :],
                    PF[:, B["yt"]:B["yt"] + 1024]), slot="by")

        sch.add("sync", sch.all_dma_done(), lambda b: None)

        slot_sems = {s: ctx.enter_context(nc.semaphore("ds_" + s))
                     for s in sch.slot_count}
        block = ctx.enter_context(nc.Block())
        sch.emit(block, sems, slot_sems)

    return nc, dbg


def make_in_map(core, inputs, ntiles=32):
    TOK = ntiles * 128
    b = core // 2
    half = core % 2
    x = np.ascontiguousarray(
        inputs["L0_attn_query"][b, half * TOK:(half + 1) * TOK, :])
    return {
        "x": x,
        "wqkv": inputs["L0_attn_W_qkv"],
        "bqkv": np.ascontiguousarray(inputs["L0_attn_b_qkv"]).reshape(1, -1),
        "wout": inputs["L0_attn_W_out"],
        "bout": np.ascontiguousarray(inputs["L0_attn_b_out"]).reshape(1, -1),
        "qw00": np.ascontiguousarray(
            inputs["L0_attn_quantum_weights"][:, 0, 0]).reshape(1, 4),
        "phases": inputs["L0_attn_photonic_phases"],
        "gates": np.ascontiguousarray(
            inputs["L0_attn_entanglement_gates"]).reshape(1, 96),
        "sup": np.ascontiguousarray(
            inputs["L0_attn_superposition_logits"]).reshape(1, 4),
        "wgate": inputs["L0_ffn_w_gate"],
        "wup": inputs["L0_ffn_w_up"],
        "wdown": inputs["L0_ffn_w_down"],
        "c_col": np.ascontiguousarray(inputs["L0_c"][b]).reshape(D, 1),
        "wmod": inputs["L0_wmod"],
    }


def kernel(**inputs):
    ntiles = 32
    nc, _ = build_program(ntiles)
    in_maps = [make_in_map(c, inputs, ntiles) for c in range(8)]
    res = run_bass_kernel_spmd(nc, in_maps, core_ids=list(range(8)))
    TOK = ntiles * 128
    out = np.empty((4, 8192, D), dtype=np.float32)
    for c in range(8):
        b, half = c // 2, c % 2
        out[b, half * TOK:(half + 1) * TOK, :] = res.results[c]["y"]
    return out


# version 13 (final)
# speedup vs baseline: 1.0211x; 1.0211x over previous
import contextlib
import numpy as np

import concourse.bass as bass
import concourse.mybir as mybir
from concourse.bass_utils import run_bass_kernel_spmd

F32 = mybir.dt.float32
BF16 = mybir.dt.bfloat16
AF = mybir.ActivationFunctionType
OP = mybir.AluOpType
AX = mybir.AxisListType

D = 1024
NH = 8
QKVD = 3072
FF = 2816
MODD = 6144
PI = float(np.pi)
TWO_PI = float(2.0 * np.pi)
ENGINES = ("sync", "tensor", "vector", "scalar", "gpsimd")


class Sched:
    def __init__(self):
        self.steps = {e: [] for e in ENGINES}
        self.count = {e: 0 for e in ENGINES}
        self.slot_count = {}

    def add(self, eng, deps, fn):
        deps = [d for d in deps if d is not None]
        self.steps[eng].append((deps, fn, False))
        self.count[eng] += 1
        return (eng, self.count[eng])

    def dma(self, deps, fn, slot="misc"):
        deps = [d for d in deps if d is not None]
        key = "dslot:" + slot
        self.steps["sync"].append((deps, fn, key))
        self.count["sync"] += 1
        self.slot_count[slot] = self.slot_count.get(slot, 0) + 1
        return (key, 16 * self.slot_count[slot])

    def dma_total(self, slot):
        return ("dtot:" + slot, 0)

    def all_dma_done(self):
        return [("dtot:" + s, 0) for s in self.slot_count]

    def emit(self, block, sems, slot_sems):
        needed = {e: set() for e in ENGINES}
        for eng in ENGINES:
            for deps, fn, is_dma in self.steps[eng]:
                for key, val in deps:
                    if key in needed:
                        needed[key].add(val)
        remap = {}
        for eng in ENGINES:
            cnt = 0
            for i in range(1, len(self.steps[eng]) + 1):
                if i in needed[eng]:
                    cnt += 1
                remap[(eng, i)] = cnt
        hwm = {}

        def mk(eng):
            def _body(b):
                ordinal = 0
                for deps, fn, is_dma in self.steps[eng]:
                    ordinal += 1
                    for key, val in deps:
                        if key == eng and val >= ordinal:
                            continue
                        if key in needed:
                            sem, v = sems[key], remap[(key, val)]
                            hkey = key
                        elif key.startswith("dslot:"):
                            sem, v, hkey = slot_sems[key[6:]], val, key
                        else:
                            s = key[5:]
                            sem = slot_sems[s]
                            v = 16 * self.slot_count[s]
                            hkey = "dslot:" + s
                        if v <= hwm.get((eng, hkey), 0):
                            continue
                        hwm[(eng, hkey)] = v
                        b.wait_ge(sem, v)
                    if is_dma:
                        fn(b).then_inc(slot_sems[is_dma[6:]], 16)
                    else:
                        r = fn(b)
                        if r is not None and ordinal in needed[eng]:
                            r.then_inc(sems[eng], 1)
            return _body

        for eng in ENGINES:
            if self.steps[eng]:
                getattr(block, eng)(mk(eng))


def build_program(ntiles, debug=(), stop_after="b"):
    _PH = ["prologue", "wprep", "a1", "a15", "a2", "b"]
    _lvl = _PH.index(stop_after)
    nc = bass.Bass("TRN2", target_bir_lowering=False,
                   detect_race_conditions=False)
    TOK = ntiles * 128
    assert ntiles % 4 == 0
    NG = ntiles // 4

    x_hbm = nc.dram_tensor("x", [TOK, D], F32, kind="ExternalInput").ap()
    wqkv_hbm = nc.dram_tensor("wqkv", [QKVD, D], F32, kind="ExternalInput").ap()
    bqkv_hbm = nc.dram_tensor("bqkv", [1, QKVD], F32, kind="ExternalInput").ap()
    wout_hbm = nc.dram_tensor("wout", [D, D], F32, kind="ExternalInput").ap()
    bout_hbm = nc.dram_tensor("bout", [1, D], F32, kind="ExternalInput").ap()
    qw00_hbm = nc.dram_tensor("qw00", [1, 4], F32, kind="ExternalInput").ap()
    phases_hbm = nc.dram_tensor("phases", [16, D], F32, kind="ExternalInput").ap()
    gates_hbm = nc.dram_tensor("gates", [1, 96], F32, kind="ExternalInput").ap()
    sup_hbm = nc.dram_tensor("sup", [1, 4], F32, kind="ExternalInput").ap()
    wgate_hbm = nc.dram_tensor("wgate", [FF, D], F32, kind="ExternalInput").ap()
    wup_hbm = nc.dram_tensor("wup", [FF, D], F32, kind="ExternalInput").ap()
    wdown_hbm = nc.dram_tensor("wdown", [D, FF], F32, kind="ExternalInput").ap()
    c_hbm = nc.dram_tensor("c_col", [D, 1], F32, kind="ExternalInput").ap()
    wmod_hbm = nc.dram_tensor("wmod", [D, MODD], F32, kind="ExternalInput").ap()

    y_hbm = nc.dram_tensor("y", [TOK, D], F32, kind="ExternalOutput").ap()
    qkv_park = nc.dram_tensor("qkv_park", [TOK, QKVD], F32).ap()
    xn_park = nc.dram_tensor("xn_park", [TOK, 2048], BF16).ap()
    x2_park = nc.dram_tensor("x2_park", [TOK, D], F32).ap()
    rowrt = nc.dram_tensor("rowrt", [1, 2048], F32).ap()
    wq_park = {
        "g": nc.dram_tensor("wgq_park", [2, D, FF], BF16).ap(),
        "u": nc.dram_tensor("wuq_park", [2, D, FF], BF16).ap(),
        "d": nc.dram_tensor("wdq_park", [2, FF, D], BF16).ap(),
    }

    dbg = {}

    def dbg_out(name, shape, dtype=F32):
        if name in debug:
            dbg[name] = nc.dram_tensor("dbg_" + name, list(shape), dtype,
                                       kind="ExternalOutput").ap()
        return dbg.get(name)

    sch = Sched()
    bank_w = [None] * 8
    bank_r = [None] * 8

    def pe_w(bank, deps, fn):
        tok = sch.add("tensor", list(deps) + [bank_r[bank]], fn)
        bank_w[bank] = tok
        return tok

    def rd_bank(eng, bank, deps, fn):
        tok = sch.add(eng, list(deps) + [bank_w[bank]], fn)
        bank_r[bank] = tok
        return tok

    ctx = contextlib.ExitStack()
    with ctx:
        consts = ctx.enter_context(nc.sbuf_tensor("consts", [128, 9216], F32)).ap()
        PF = ctx.enter_context(nc.sbuf_tensor("pf", [128, 28672], F32)).ap()
        PB = ctx.enter_context(nc.sbuf_tensor("pbf", [128, 28672], BF16)).ap()
        ident = ctx.enter_context(nc.sbuf_tensor("ident", [128, 128], F32)).ap()
        pb = [ctx.enter_context(nc.psum_tensor(f"pb{i}", [128, 512], F32)).ap()
              for i in range(8)]
        sems = {e: ctx.enter_context(nc.semaphore("sem_" + e)) for e in ENGINES}

        CO = {}
        _c = [0]
        for nm, n in [("cos", 128), ("sin", 128), ("ga", 1024), ("gb", 1024),
                      ("cb1", 1024), ("sb_", 1024), ("cols", 256), ("ones", 128),
                      ("b_ln", 128), ("rows", 4352)]:
            CO[nm] = _c[0]
            _c[0] += n
        assert _c[0] <= 9216

        def cv(name, n):
            return consts[:, CO[name]:CO[name] + n]

        COLS = {}
        _cc = [CO["cols"]]
        for nm, n in [("ca1", 8), ("sa", 8), ("A", 21), ("B", 21), ("C", 21),
                      ("gam", 4), ("supw", 4), ("silc", 8)]:
            COLS[nm] = _cc[0]
            _cc[0] += n
        assert _cc[0] <= CO["cols"] + 256

        def col(name, n=1, j=0):
            return consts[:, COLS[name] + j:COLS[name] + j + n]

        ROWS0 = CO["rows"]

        def rwc(off, n):
            assert off + n <= 4352
            return consts[0:1, ROWS0 + off:ROWS0 + off + n]
        biasp_row = rwc(0, QKVD)
        bout_row = rwc(QKVD, D)
        ones_row = consts[0:1, CO["ones"]:CO["ones"] + 128]

        def pr(off, n):
            assert off + n <= 8192
            return PF[0:1, off:off + n]

        dummy_row = rwc(4128, 128)
        t_dz = sch.add("gpsimd", [], lambda g: g.memset(dummy_row, 0.0))
        t_pre1 = sch.add("scalar", [t_dz], lambda s: s.preload_act_table(AF.Silu))
        for _ in range(3):
            t_pre1 = sch.add("scalar", [t_pre1], lambda s: s.activation(
                dummy_row, dummy_row, AF.Silu))
        t_id0 = sch.add("gpsimd", [], lambda g: g.memset(ident[:], 1.0))
        t_id = sch.add("gpsimd", [t_id0], lambda g: g.affine_select(
            out=ident[:], in_=ident[:], compare_op=OP.is_equal, fill=0.0,
            base=0, pattern=[[-1, 128]], channel_multiplier=1))
        t_ones = sch.add("gpsimd", [], lambda g: g.memset(ones_row, 1.0))

        sch.dma([], lambda b: b.dma_start(
            col("silc", 8), bass.AP(c_hbm.tensor, 0, [[1, 128], [128, 8]]),
            allow_slow_non_contiguous=True), slot="misc")
        for r in range(16):
            sch.dma([], lambda b, r=r: b.dma_start(
                pr(0, 128)[:, r:128:16], phases_hbm[r:r + 1, r:r + 128:16],
                allow_slow_non_contiguous=True), slot="misc")
        sch.dma([], lambda b: b.dma_start(pr(128, 96), gates_hbm), slot="misc")
        sch.dma([], lambda b: b.dma_start(pr(224, 4), qw00_hbm), slot="misc")
        sch.dma([], lambda b: b.dma_start(pr(228, 4), sup_hbm), slot="misc")
        sch.dma([], lambda b: b.dma_start(biasp_row, bqkv_hbm), slot="misc")
        sch.dma([], lambda b: b.dma_start(bout_row, bout_hbm), slot="misc")
        d_c = d_ph = d_gates = d_qw = d_sup = d_bq = d_bo = sch.dma_total("misc")

        t_silc = sch.add("scalar", [d_c], lambda s: s.activation(
            col("silc", 8), col("silc", 8), AF.Silu))

        TP1 = float(np.float32(TWO_PI))
        TP2 = float(TWO_PI - TP1)
        HP1 = float(np.float32(PI / 2))
        HP2 = float(PI / 2 - HP1)

        def emit_wrap(dst, src, n, half_shift):
            if half_shift:
                tok = sch.add("vector", [d_ph], lambda v: v.tensor_scalar(
                    pr(dst, n), pr(src, n), HP1, None, OP.add))
                tok = sch.add("vector", [tok], lambda v: v.tensor_scalar(
                    pr(dst, n), pr(dst, n), HP2, None, OP.add))
            else:
                tok = sch.add("vector", [d_ph], lambda v: v.tensor_copy(
                    pr(dst, n), pr(src, n)))
            for _ in range(8):
                sch.add("vector", [], lambda v: v.tensor_scalar(
                    pr(512, n), pr(dst, n), PI, None, OP.is_gt))
                sch.add("vector", [], lambda v: v.scalar_tensor_tensor(
                    pr(dst, n), pr(512, n), -TP1, pr(dst, n), OP.mult, OP.add))
                sch.add("vector", [], lambda v: v.scalar_tensor_tensor(
                    pr(dst, n), pr(512, n), -TP2, pr(dst, n), OP.mult, OP.add))
                sch.add("vector", [], lambda v: v.tensor_scalar(
                    pr(512, n), pr(dst, n), -PI, None, OP.is_lt))
                sch.add("vector", [], lambda v: v.scalar_tensor_tensor(
                    pr(dst, n), pr(512, n), TP1, pr(dst, n), OP.mult, OP.add))
                tok = sch.add("vector", [], lambda v: v.scalar_tensor_tensor(
                    pr(dst, n), pr(512, n), TP2, pr(dst, n), OP.mult, OP.add))
            return tok

        t_w1 = emit_wrap(256, 0, 128, True)
        t_w2 = emit_wrap(384, 0, 128, False)
        t_cosr = sch.add("scalar", [t_w1], lambda s: s.activation(
            pr(256, 128), pr(256, 128), AF.Sin))
        t_sinr = sch.add("scalar", [t_w2], lambda s: s.activation(
            pr(384, 128), pr(384, 128), AF.Sin))

        gates4 = pr(128, 96).rearrange("o (d i f) -> o d i f", d=3, i=8)
        t_th = sch.add("vector", [d_gates], lambda v: v.tensor_scalar(
            pr(640, 21).rearrange("o (d i f) -> o d i f", d=3, i=7),
            gates4[:, :, 0:7, 0:1], 0.5, None, OP.mult))
        t_phi = sch.add("vector", [d_gates], lambda v: v.tensor_copy(
            pr(672, 21).rearrange("o (d i f) -> o d i f", d=3, i=7),
            gates4[:, :, 0:7, 1:2]))
        t_ct0 = sch.add("vector", [t_th], lambda v: v.tensor_scalar(
            pr(704, 21), pr(640, 21), HP1, None, OP.add))
        t_ct0 = sch.add("vector", [t_ct0], lambda v: v.tensor_scalar(
            pr(704, 21), pr(704, 21), HP2, None, OP.add))
        t_ct = sch.add("scalar", [t_ct0], lambda s: s.activation(
            pr(704, 21), pr(704, 21), AF.Sin))
        t_st = sch.add("scalar", [t_th], lambda s: s.activation(
            pr(736, 21), pr(640, 21), AF.Sin))
        t_cp0 = sch.add("vector", [t_phi], lambda v: v.tensor_scalar(
            pr(768, 21), pr(672, 21), HP1, None, OP.add))
        t_cp0 = sch.add("vector", [t_cp0], lambda v: v.tensor_scalar(
            pr(768, 21), pr(768, 21), HP2, None, OP.add))
        t_cp = sch.add("scalar", [t_cp0], lambda s: s.activation(
            pr(768, 21), pr(768, 21), AF.Sin))
        t_sp = sch.add("scalar", [t_phi], lambda s: s.activation(
            pr(800, 21), pr(672, 21), AF.Sin))
        t_a1 = sch.add("vector", [t_ct, t_st, t_cp, t_sp],
                       lambda v: v.tensor_tensor(pr(832, 21), pr(736, 21),
                                                 pr(768, 21), OP.mult))
        t_B = sch.add("vector", [t_sp], lambda v: v.tensor_tensor(
            pr(864, 21), pr(736, 21), pr(800, 21), OP.mult))
        t_A = sch.add("vector", [t_a1], lambda v: v.scalar_tensor_tensor(
            pr(896, 21), pr(704, 21), -1.0, pr(832, 21), OP.add, OP.add))
        t_C0 = sch.add("vector", [t_a1], lambda v: v.tensor_tensor(
            pr(928, 21), pr(704, 21), pr(832, 21), OP.subtract))
        t_C = sch.add("vector", [t_C0], lambda v: v.tensor_scalar(
            pr(928, 21), pr(928, 21), -1.0, None, OP.add))

        t_rat = sch.add("scalar", [d_qw], lambda s: s.activation(
            pr(960, 4), pr(224, 4), AF.Sin))
        R31 = float(np.float32(0.3))
        R32 = float(0.3 - R31)
        t_r2 = sch.add("vector", [t_rat], lambda v: v.tensor_scalar(
            pr(980, 4), pr(960, 4), R32, None, OP.mult))
        t_r2 = sch.add("vector", [t_r2], lambda v: v.tensor_scalar(
            pr(960, 4), pr(960, 4), R31, 0.5, OP.mult, OP.add))
        t_r2 = sch.add("vector", [t_r2], lambda v: v.tensor_tensor(
            pr(960, 4), pr(960, 4), pr(980, 4), OP.add))
        t_1m = sch.add("vector", [t_r2], lambda v: v.tensor_scalar(
            pr(964, 4), pr(960, 4), -1.0, 1.0, OP.mult, OP.add))
        t_g0 = sch.add("vector", [t_1m], lambda v: v.tensor_tensor(
            pr(968, 4), pr(960, 4), pr(964, 4), OP.mult))
        t_gam = sch.add("vector", [t_g0], lambda v: v.tensor_scalar(
            pr(968, 4), pr(968, 4), float(128 ** -0.5), None, OP.mult))

        def load_lnexp_set(s):
            return s.add_instruction(mybir.InstLoadActFuncSet(
                name=s.bass.get_next_instruction_name(),
                act_func_set_id=6, ins=[], outs=[]))
        t_pre2 = sch.add("scalar", [t_rat, t_cosr, t_sinr, t_ct, t_st,
                                    t_cp, t_sp, t_silc], load_lnexp_set)
        for _ in range(3):
            t_pre2 = sch.add("scalar", [t_pre2], lambda s: s.activation(
                dummy_row, dummy_row, AF.Exp))
        t_se = sch.add("scalar", [d_sup, t_pre2],
                       lambda s: s.activation(pr(972, 4), pr(228, 4), AF.Exp))
        t_ssum = sch.add("vector", [t_se], lambda v: v.tensor_reduce(
            pr(976, 1), pr(972, 4), AX.X, OP.add))
        t_sr = sch.add("vector", [t_ssum], lambda v: v.reciprocal(
            pr(977, 1), pr(976, 1)))
        t_sw = sch.add("vector", [t_sr], lambda v: v.tensor_scalar(
            pr(972, 4), pr(972, 4), pr(977, 1), None, OP.mult))

        def bcast(dst, src_row, deps):
            n = src_row.shape[-1]
            tok = None
            for o in range(0, n, 512):
                w = min(512, n - o)
                mt = pe_w(7, list(deps) + [t_ones],
                          lambda t, o=o, w=w: t.matmul(
                              pb[7][:, 0:w], ones_row, src_row[:, o:o + w],
                              start=True, stop=True))
                tok = rd_bank("scalar", 7, [], lambda s, o=o, w=w: s.activation(
                    dst[:, o:o + w], pb[7][:, 0:w], AF.Copy))
            return tok

        bcast(cv("cos", 128), pr(256, 128), [t_cosr])
        bcast(cv("sin", 128), pr(384, 128), [t_sinr])
        t_bcA = bcast(col("A", 21), pr(896, 21), [t_A])
        t_bcB = bcast(col("B", 21), pr(864, 21), [t_B])
        t_bcC = bcast(col("C", 21), pr(928, 21), [t_C])
        t_bcgam = bcast(col("gam", 4), pr(968, 4), [t_gam])
        t_bcsup = bcast(col("supw", 4), pr(972, 4), [t_sw])

        MODR = 1024
        WMB = [8192, 8192 + 4096]

        def wmb(i, n):
            return PF[:, WMB[i]:WMB[i] + n]

        mod_parts = []
        last_mm = [None, None]
        for c0, nw in [(0, 8), (4096, 4)]:
            for kc in range(8):
                def dl(b, kc=kc, c0=c0, nw=nw):
                    return b.dma_start(wmb(kc % 2, nw * 512),
                                       wmod_hbm[kc * 128:(kc + 1) * 128,
                                                c0:c0 + nw * 512])
                dtok = sch.dma([t_silc, last_mm[kc % 2]], dl,
                               slot="wm%d" % (kc % 2))

                def mmstep(t, kc=kc, nw=nw):
                    last = None
                    for j in range(nw):
                        last = t.matmul(pb[j][0:1, :], col("silc", 8)[:, kc:kc + 1],
                                        wmb(kc % 2, nw * 512)[:, j * 512:(j + 1) * 512],
                                        start=(kc == 0), stop=(kc == 7))
                    return last
                deps = [dtok]
                if kc == 0:
                    deps += [bank_r[j] for j in range(nw)]
                tok = sch.add("tensor", deps, mmstep)
                for j in range(nw):
                    bank_w[j] = tok
                last_mm[kc % 2] = tok
            for j in range(nw):
                mod_parts.append(rd_bank("vector", j, [], lambda v, j=j, c0=c0:
                                 v.tensor_copy(pr(MODR + c0 + j * 512, 512),
                                               pb[j][0:1, :])))
        t_mod = mod_parts[-1]
        t_mod_all = sch.add("vector", mod_parts, lambda v: v.memset(consts[0:1, CO["rows"] + 4200:CO["rows"] + 4201], 0.0))

        if dbg_out("mod", (1, MODD)):
            sch.dma([t_mod_all], lambda b: b.dma_start(dbg["mod"], pr(MODR, MODD)),
                    slot="dbg1")

        def mseg(i):
            return pr(MODR + i * 1024, 1024)

        sgn_row = PF[0:1, 8192:8192 + 1024]
        abs_row = PF[0:1, 9216:9216 + 1024]
        prow_bc_toks = []
        prev_bc = None
        for i, nm in ((2, "ga"), (5, "gb")):
            ts1 = sch.add("vector", [t_mod_all, prev_bc], lambda v, i=i:
                          v.tensor_scalar(sgn_row, mseg(i), 0.0, 2.0,
                                          OP.is_ge, OP.mult))
            ts2 = sch.add("vector", [ts1], lambda v: v.tensor_scalar(
                sgn_row, sgn_row, -1.0, None, OP.add))
            ta = sch.add("scalar", [t_mod_all, prev_bc], lambda s, i=i:
                         s.activation(abs_row, mseg(i), AF.Abs))
            tm = sch.add("vector", [ta], lambda v: v.tensor_scalar(
                abs_row, abs_row, 0.25, None, OP.max))
            tg = sch.add("vector", [ts2, tm], lambda v: v.tensor_tensor(
                sgn_row, sgn_row, abs_row, OP.mult))
            prev_bc = bcast(cv(nm, 1024), sgn_row, [tg])
            prow_bc_toks.append(prev_bc)

        t_cb1 = sch.add("vector", [t_mod_all, prev_bc], lambda v: v.tensor_scalar(
            abs_row, mseg(4), 1.0, None, OP.add))
        prow_bc_toks.append(bcast(cv("cb1", 1024), abs_row, [t_cb1]))
        prow_bc_toks.append(bcast(cv("sb_", 1024), mseg(3), [t_mod_all]))

        t_ca1r = sch.add("vector", [t_mod_all], lambda v: v.tensor_scalar(
            pr(2048, 1024), mseg(1), 1.0, None, OP.add))
        d_w1 = sch.dma([t_ca1r], lambda b: b.dma_start(
            rowrt[:, 0:1024], pr(2048, 1024)), slot="rw1")
        d_r1 = sch.dma([d_w1], lambda b: b.dma_start(
            col("ca1", 8), bass.AP(rowrt.tensor, 0, [[1, 128], [128, 8]]),
            allow_slow_non_contiguous=True), slot="rr1")
        d_w2 = sch.dma([t_mod_all], lambda b: b.dma_start(
            rowrt[:, 1024:2048], mseg(0)), slot="rw2")
        d_r2 = sch.dma([d_w2], lambda b: b.dma_start(
            col("sa", 8), bass.AP(rowrt.tensor, 1024, [[1, 128], [128, 8]]),
            allow_slow_non_contiguous=True), slot="rr2")

        prologue_done = sch.add("vector", [d_r1, d_r2, t_se, t_sw, t_bcA, t_bcB,
                                           t_bcC, t_bcgam, t_bcsup, t_mod_all]
                                + prow_bc_toks, lambda v: v.memset(consts[0:1, CO["rows"] + 4200:CO["rows"] + 4201], 0.0))

        def emit_quant(src, dst, n, scr, deps):
            nb = n // 16
            zn, az, acc, sg, am, rs = (scr["zn"][:, 0:n], scr["az"][:, 0:n],
                                       scr["acc"][:, 0:n], scr["sg"][:, 0:n],
                                       scr["am"][:, 0:nb], scr["rs"][:, 0:nb])

            def bview(a):
                return a.rearrange("p (b e) -> p e b", e=16)

            def sview(s):
                return s.unsqueeze(1).broadcast_to([128, 16, nb])

            t0 = sch.add("vector", deps, lambda v: v.tensor_reduce(
                am, src.rearrange("p (b e) -> p b e", e=16), AX.X, OP.max,
                apply_absolute_value=True))
            t1 = sch.add("vector", [t0], lambda v: v.tensor_scalar(
                am, am, 6e-12, None, OP.max))
            t2 = sch.add("vector", [t1], lambda v: v.reciprocal(rs, am))
            t3 = sch.add("vector", [t2], lambda v: v.tensor_tensor(
                bview(zn), bview(src), sview(rs), OP.mult))
            t4 = sch.add("scalar", [t3], lambda s: s.activation(az, zn, AF.Abs,
                                                                scale=6.0))
            t5 = sch.add("scalar", [t3], lambda s: s.sign(sg, zn))
            t2b = sch.add("vector", [t3], lambda v: v.tensor_scalar(
                rs, am, 1.0 / 6.0, None, OP.mult))
            tc = sch.add("vector", [t4], lambda v: v.tensor_scalar(
                acc, az, 0.25, None, OP.is_gt))
            for thr in (0.75, 1.25, 1.75, 2.5, 3.5, 5.0):
                tc = sch.add("vector", [tc], lambda v, thr=thr:
                             v.scalar_tensor_tensor(acc, az, float(thr), acc,
                                                    OP.is_gt, OP.add))
            t6 = sch.add("vector", [tc, t5], lambda v: v.tensor_scalar(
                az, acc, -4.0, 0.0, OP.add, OP.max))
            t7 = sch.add("vector", [t6], lambda v: v.tensor_scalar(
                zn, acc, -6.0, 0.0, OP.add, OP.max))
            t8 = sch.add("vector", [t7], lambda v: v.scalar_tensor_tensor(
                acc, acc, 0.5, zn, OP.mult, OP.add))
            t9 = sch.add("vector", [t8], lambda v: v.scalar_tensor_tensor(
                acc, az, 0.5, acc, OP.mult, OP.add))
            ta_ = sch.add("vector", [t9], lambda v: v.tensor_tensor(
                acc, acc, sg, OP.mult))
            tb_ = sch.add("vector", [ta_, t2b], lambda v: v.tensor_tensor(
                bview(dst), bview(acc), sview(rs), OP.mult))
            return tb_

        tp_state = {"i": 0}

        def emit_tp(src_chunk, hi_dst, lo_dst, deps):
            bank = 4 + (tp_state["i"] % 2)
            tp_state["i"] += 1
            ptok = pe_w(bank, deps, lambda t: t.transpose(
                pb[bank][:, 0:128], src_chunk, ident[:]))
            htok = sch.add("scalar", [ptok], lambda s: s.activation(
                hi_dst, pb[bank][:, 0:128], AF.Copy))
            ltok = rd_bank("vector", bank, [htok], lambda v: v.tensor_tensor(
                lo_dst, pb[bank][:, 0:128], hi_dst, OP.subtract))
            return ltok

        WLD = [8192, 8192 + 1024]
        STG = 8192 + 2048

        def wqkv_prep_half(half):
            toks = []
            stg = PF[:, STG:STG + 128]
            prev_stg_readers = []
            bias_ev_prev = None
            for nt in range(12):
                ntg = half * 12 + nt
                dtok = sch.dma([prologue_done] + prev_stg_readers,
                               lambda b, ntg=ntg, nt=nt: b.dma_start(
                                   PF[:, WLD[nt % 2]:WLD[nt % 2] + 1024],
                                   wqkv_hbm[ntg * 128:(ntg + 1) * 128, :]),
                               slot="wld%d" % (nt % 2))
                bb = 6 + nt % 2
                for kc in range(8):
                    ptok = pe_w(4 + kc % 2, [dtok], lambda t, nt=nt, kc=kc:
                                t.transpose(pb[4 + kc % 2][:, 0:128],
                                            PF[:, WLD[nt % 2] + kc * 128:
                                               WLD[nt % 2] + kc * 128 + 128],
                                            ident[:]))
                    etok = rd_bank("scalar", 4 + kc % 2, prev_stg_readers,
                                   lambda s, kc=kc: s.activation(
                                       stg, pb[4 + kc % 2][:, 0:128], AF.Copy))
                    btok = pe_w(bb, [etok, d_r2], lambda t, kc=kc, nt=nt, bb=bb:
                                t.matmul(pb[bb][0:1, 0:128],
                                         col("sa", 8)[:, kc:kc + 1], stg,
                                         start=(kc == 0), stop=(kc == 7)))
                    hoff = kc * 1536 + nt * 128
                    hitok = sch.add("vector", [etok, d_r1], lambda v, hoff=hoff, kc=kc:
                                    v.tensor_scalar(PB[:, hoff:hoff + 128], stg,
                                                    col("ca1", 8)[:, kc:kc + 1],
                                                    None, OP.mult))
                    lotok = sch.add("vector", [hitok], lambda v, hoff=hoff, kc=kc:
                                    v.scalar_tensor_tensor(
                                        PB[:, 12288 + hoff:12288 + hoff + 128],
                                        stg, col("ca1", 8)[:, kc:kc + 1],
                                        PB[:, hoff:hoff + 128],
                                        OP.mult, OP.subtract))
                    prev_stg_readers = [btok, lotok]
                bev = rd_bank("vector", bb, [d_bq], lambda v, ntg=ntg, bb=bb:
                              v.scalar_tensor_tensor(
                                  biasp_row[:, ntg * 128:ntg * 128 + 128],
                                  pb[bb][0:1, 0:128], 1.0,
                                  biasp_row[:, ntg * 128:ntg * 128 + 128],
                                  OP.mult, OP.add))
                toks.append(bev)
                toks.append(prev_stg_readers[1])
            return sch.add("vector", toks, lambda v: v.memset(consts[0:1, CO["rows"] + 4200:CO["rows"] + 4201], 0.0))

        WQT = 8192 + 2176

        def wqr(name):
            offs = {"wt0": 0, "wt1": 2816, "zn": 5632, "az": 8448,
                    "acc": 11264, "sg": 14080, "am": 16896, "rs": 17072}
            sizes = {"wt0": 2816, "wt1": 2816, "zn": 2816, "az": 2816,
                     "acc": 2816, "sg": 2816, "am": 176, "rs": 176}
            return PF[:, WQT + offs[name]:WQT + offs[name] + sizes[name]]

        wq_scr = {k: wqr(k) for k in ("zn", "az", "acc", "sg", "am", "rs")}
        wq_tiles = []
        for key, hbm, ktot in (("g", wgate_hbm, D), ("u", wup_hbm, D),
                               ("d", wdown_hbm, FF)):
            nrows = (FF if key != "d" else D) // 128
            for nt in range(nrows):
                wq_tiles.append((key, hbm, nt, ktot))
        wq_state = {"prev": [None, None], "stage_prev": [None, None]}
        WQS = [24576, 24576 + 256]

        def emit_wq_tile(idx):
            key, hbm, nt, ktot = wq_tiles[idx]
            buf = wqr("wt0") if idx % 2 == 0 else wqr("wt1")
            buf = buf[:, 0:ktot]
            dtok = sch.dma([wq_state["prev"][idx % 2]],
                           lambda b: b.dma_start(buf, hbm[nt * 128:(nt + 1) * 128, :]),
                           slot="wq%d" % (idx % 2))
            qtok = emit_quant(buf, buf, ktot, wq_scr,
                              [dtok, wq_state["prev"][1 - idx % 2]])
            wq_state["prev"][idx % 2] = qtok
            park = wq_park[key]
            NTOT = FF if key != "d" else D
            last = None
            for kc in range(ktot // 128):
                so = WQS[kc % 2]
                hi = PB[:, so:so + 128]
                lo = PB[:, so + 128:so + 256]
                ltok = emit_tp(buf[:, kc * 128:(kc + 1) * 128], hi, lo,
                               [qtok, wq_state["stage_prev"][kc % 2]])
                sch.dma([ltok], lambda b, kc=kc, nt=nt, park=park, NTOT=NTOT,
                        hi=hi: b.dma_start(
                            bass.AP(park.tensor, kc * 128 * NTOT + nt * 128,
                                    [[NTOT, 128], [1, 128]]), hi),
                        slot="wqp%d" % (kc % 2))
                dpk = sch.dma([ltok], lambda b, kc=kc, nt=nt, park=park,
                              NTOT=NTOT, lo=lo: b.dma_start(
                                  bass.AP(park.tensor,
                                          D * FF + kc * 128 * NTOT + nt * 128,
                                          [[NTOT, 128], [1, 128]]), lo),
                              slot="wqp%d" % (kc % 2))
                wq_state["stage_prev"][kc % 2] = dpk
                last = dpk
            return last

        XB = [0, 1024]
        JUNK = 2048
        XN = 3072
        LNC = 4096
        QSB = [4112, 4112 + 1536]

        def lnc(i):
            return PF[:, LNC + i:LNC + i + 1]

        a1_state = {"xcons": [prologue_done, prologue_done],
                    "qcons": [None, None], "ncev": {}}
        wq_emitted = [0]
        wq_last = [None]

        def emit_a1_tile(i, half):
            XF = 26624
            if half == 1:
                ev = sch.dma([a1_state.get("xfguard"),
                              sch.dma_total("xnp0"), sch.dma_total("xnp1")],
                             lambda b, i=i: b.dma_start(
                                 PB[:, XF:XF + 2048],
                                 xn_park[i * 128:(i + 1) * 128, :]),
                             slot="xnl%d" % (i % 2))
                return emit_a1_mms(i, half, ev)
            dx = sch.dma([a1_state["xcons"][i % 2]], lambda b, i=i: b.dma_start(
                PF[:, XB[i % 2]:XB[i % 2] + 1024], x_hbm[i * 128:(i + 1) * 128, :]),
                slot="xb%d" % (i % 2))
            xb = PF[:, XB[i % 2]:XB[i % 2] + 1024]
            junk = PF[:, JUNK + (i % 2) * 1024:JUNK + (i % 2) * 1024 + 1024]
            p8 = 8 * (i % 2)

            def lnp(k, p8=p8):
                return lnc(p8 + k)
            t1 = sch.add("scalar", [dx], lambda s: s.activation(
                junk, xb, AF.Copy, accum_out=lnp(0)))
            t2 = sch.add("scalar", [t1], lambda s: s.activation(
                lnp(1), lnp(0), AF.Copy, scale=-1.0 / D))
            t3 = sch.add("scalar", [t2], lambda s: s.activation(
                junk, xb, AF.Square, bias=lnp(1), accum_out=lnp(2)))
            t4 = sch.add("scalar", [t3], lambda s: s.activation(
                lnp(3), lnp(2), AF.Copy, scale=1.0 / D, bias=1e-5))
            t5 = sch.add("scalar", [t4], lambda s: s.activation(
                lnp(4), lnp(3), AF.Ln))
            t6 = sch.add("scalar", [t5], lambda s: s.activation(
                lnp(5), lnp(4), AF.Exp, scale=-0.5))
            t7 = sch.add("vector", [t6], lambda v: v.tensor_scalar(
                lnp(6), lnp(1), lnp(5), None, OP.mult))
            t8 = sch.add("vector", [t7], lambda v: v.tensor_scalar(
                xb, xb, lnp(5), lnp(6), OP.mult, OP.add))
            xn = xb
            if i == 0 and half == 0 and dbg_out("ln", (128, 8)):
                sch.dma([t8], lambda b: b.dma_start(
                    dbg["ln"], PF[:, LNC:LNC + 8]), slot="dbg1")
            if i == 0 and half == 0 and dbg_out("lnx", (128, 1024)):
                sch.dma([t8], lambda b: b.dma_start(dbg["lnx"], xn),
                        slot="dbg1")
            if _noln:
                a1_state["xcons"][i % 2] = t8
                return t8
            ev = None
            for kc in range(8):
                ev = emit_tp(xn[:, kc * 128:(kc + 1) * 128],
                             PB[:, XF + kc * 128:XF + kc * 128 + 128],
                             PB[:, XF + 1024 + kc * 128:XF + 1024 + kc * 128 + 128],
                             [t8, a1_state.get("xfguard"),
                              a1_state.get("xfpark")])
            a1_state["xcons"][i % 2] = ev
            if _nomm:
                return ev
            a1_state["xfpark"] = sch.dma([ev], lambda b, i=i: b.dma_start(
                xn_park[i * 128:(i + 1) * 128, :], PB[:, XF:XF + 2048]),
                slot="xnp%d" % (i % 2))
            return emit_a1_mms(i, half, ev)

        def emit_a1_mms(i, half, ev):
            XF = 26624
            _onemm = bool(_os.environ.get("K_ONE_MM"))
            _nnch = int(_os.environ.get("K_NCH", "3"))
            qdep = a1_state["qcons"][i % 2]
            _bof = int(_os.environ.get("K_BOFF", "1"))
            for nch in range(1 if _onemm else _nnch):
                bank = (nch % 2) * _bof
                gch = half * 3 + nch

                def mmall(t, nch=nch, bank=bank):
                    last = None
                    for kc in range(8):
                        xh = PB[:, XF + kc * 128:XF + kc * 128 + 128]
                        xl = PB[:, XF + 1024 + kc * 128:XF + 1024 + kc * 128 + 128]
                        wh = PB[:, kc * 1536 + nch * 512:kc * 1536 + nch * 512 + 512]
                        wl = PB[:, 12288 + kc * 1536 + nch * 512:
                                12288 + kc * 1536 + nch * 512 + 512]
                        t.matmul(pb[bank], xh, wh, start=(kc == 0), stop=False)
                        t.matmul(pb[bank], xh, wl, start=False, stop=False)
                        last = t.matmul(pb[bank], xl, wh, start=False, stop=False)
                    return last
                import os as _os2
                _ser = bool(_os2.environ.get("K_SER_EVAC"))
                ob = _bof - bank if bank else _bof
                mtok = pe_w(bank, [ev, qdep] +
                            ([a1_state["ncev"].get(ob)] if _ser else []),
                            mmall)

                def bmm(t, bank=bank, gch=gch):
                    if _nobias:
                        return t.matmul(pb[bank],
                                        PB[:, 0:128], PB[:, 0:512],
                                        start=False, stop=True)
                    return t.matmul(pb[bank], ones_row,
                                    biasp_row[:, gch * 512:(gch + 1) * 512],
                                    start=False, stop=True)
                btok = sch.add("tensor", [t_ones], bmm)
                bank_w[bank] = btok
                a1_state["xfguard"] = btok
                etok = rd_bank("scalar", bank, [],
                               lambda s, nch=nch, i=i, bank=bank:
                               s.activation(PF[:, QSB[i % 2] + nch * 512:
                                               QSB[i % 2] + (nch + 1) * 512],
                                            pb[bank], AF.Copy))
                a1_state["ncev"][bank] = etok
            dq = sch.dma([a1_state["ncev"].get(0),
                          a1_state["ncev"].get(_bof) or
                          a1_state["ncev"].get(0)],
                         lambda b, i=i, half=half: b.dma_start(
                             qkv_park[i * 128:(i + 1) * 128,
                                      half * 1536:(half + 1) * 1536],
                             PF[:, QSB[i % 2]:QSB[i % 2] + 1536]),
                         slot="qp%d" % (i % 2))
            a1_state["qcons"][i % 2] = dq
            return dq

        if _lvl == 1:
            wqkv_prep_half(0)
        import os as _os
        _nowq = bool(_os.environ.get("K_NO_WQ"))
        _notp = bool(_os.environ.get("K_NO_TILES"))
        _nomm = bool(_os.environ.get("K_NO_MM"))
        _nobias = bool(_os.environ.get("K_NO_BIAS"))
        _noln = bool(_os.environ.get("K_NO_LNTP"))
        last_a1 = None
        for half in (range(2) if _lvl >= 2 else ()):
            wp_tok = wqkv_prep_half(half)
            a1_state["qcons"] = [wp_tok, wp_tok]
            for i in range(ntiles if not _notp else 0):
                last_a1 = emit_a1_tile(i, half)
                if not _nowq and wq_emitted[0] < len(wq_tiles):
                    wq_last[0] = emit_wq_tile(wq_emitted[0])
                    wq_emitted[0] += 1
        while _lvl >= 2 and not _nowq and wq_emitted[0] < len(wq_tiles):
            wq_last[0] = emit_wq_tile(wq_emitted[0])
            wq_emitted[0] += 1

        if dbg_out("qkv", (TOK, QKVD)):
            sch.dma([last_a1, sch.dma_total("qp0"), sch.dma_total("qp1")],
                    lambda b: b.dma_start(dbg["qkv"], qkv_park),
                    slot="dbg2")
        if dbg_out("wgq", (2, D, FF), BF16):
            sch.dma([wq_last[0], sch.dma_total("wqp0"), sch.dma_total("wqp1")],
                    lambda b: b.dma_start(dbg["wgq"], wq_park["g"]), slot="dbg3")

        a15_prev = [last_a1, last_a1]
        a15_readers = []
        for nt in (range(8) if _lvl >= 3 else ()):
            dtok = sch.dma([a15_prev[nt % 2], wq_last[0]],
                           lambda b, nt=nt: b.dma_start(
                               PF[:, WLD[nt % 2]:WLD[nt % 2] + 1024],
                               wout_hbm[nt * 128:(nt + 1) * 128, :]),
                           slot="wo%d" % (nt % 2))
            ltok = None
            for kc in range(8):
                hoff = kc * 1024 + nt * 128
                ltok = emit_tp(PF[:, WLD[nt % 2] + kc * 128:
                                  WLD[nt % 2] + kc * 128 + 128],
                               PB[:, hoff:hoff + 128],
                               PB[:, 8192 + hoff:8192 + hoff + 128], [dtok])
            a15_prev[nt % 2] = ltok
            a15_readers.append(ltok)
        t_wo = sch.add("vector", a15_readers, lambda v: v.memset(consts[0:1, CO["rows"] + 4200:CO["rows"] + 4201], 0.0)) \
            if _lvl >= 3 else None

        A2 = dict(qsb0=0, qsb1=3072, sq=6144, xb2=6144 + 2048 + 1024,
                  qe=9216, ke=11264, vq=13312, pp0=15360, pp1=17152,
                  uw=18944, pch=19200, scrc=19456, scr2=19584, nrm=19840,
                  vsum=19968, m2c=20224, rinv=20352, coh=20480, P=20736,
                  s2=20864, ec=20928, den=20992, wmat=21008, wt=21072,
                  m2w=23120, rr6=24144, fac=25168, fin=26192)
        A2["gsc"] = A2["sq"]
        A2["xb2"] = A2["sq"] + 2048
        A2["m1s"] = A2["rr6"]
        A2["m2s"] = A2["m2w"]
        A2["outr"] = A2["fac"]

        def pa(name, n):
            return PF[:, A2[name]:A2[name] + n]

        def nrmc(i, n=1):
            return PF[:, A2["nrm"] + i:A2["nrm"] + i + n]

        XF2 = 16384

        a2_state = {"qcons": [t_wo, t_wo], "x2cons": None, "xb2cons": None,
                    "finguard": t_wo}
        a2_lnstash = []
        a2_parks = []

        def entangle_state(zkey, sqoff, start_deps):
            zbuf = pa(zkey, 2048)
            pp0 = pa("pp0", 1792)
            pp1 = pa("pp1", 1792)
            scrc7 = PF[:, A2["uw"]:A2["uw"] + 896]
            bufs = [zbuf, pp0, pp1, zbuf]
            uwbufs = [pp1, zbuf, pp0]
            tokv = None
            for r in range(3):
                src, dst = bufs[r], bufs[r + 1]
                uw7 = uwbufs[r]
                scr2x = dst[:, 0:1792]

                def s2_of(i, r=r):
                    if i == 6:
                        return zbuf[:, 7 * 256:8 * 256]
                    if r == 0:
                        return zbuf[:, (i + 1) * 256:(i + 2) * 256]
                    return src[:, (i + 1) * 256:(i + 2) * 256]

                if r == 0:
                    for i in range(7):
                        tctrl = sch.add("vector",
                                        start_deps if i == 0 else [],
                                        lambda v, i=i: v.tensor_scalar(
                                            scrc7[:, i * 128:(i + 1) * 128],
                                            PF[:, A2["sq"] + sqoff + i * 128:
                                               A2["sq"] + sqoff + i * 128 + 128],
                                            nrmc(40 + (sqoff // 1024) * 8 + i),
                                            None, OP.mult))
                else:
                    tsqr = sch.add("scalar", [tokv], lambda s, src=src,
                                   scr2x=scr2x: s.activation(
                                       scr2x, src[:, 0:1792], AF.Square))
                    tctrl = sch.add("vector", [tsqr], lambda v, scr2x=scr2x:
                                    v.tensor_tensor(
                                        scrc7.rearrange("p (i d) -> p i d", i=7),
                                        scr2x.rearrange("p (i c d) -> p i c d",
                                                        i=7, c=2)[:, :, 0, :],
                                        scr2x.rearrange("p (i c d) -> p i c d",
                                                        i=7, c=2)[:, :, 1, :],
                                        OP.add))
                sc_toks = []
                for i in range(7):
                    ci = r * 7 + i
                    s2 = s2_of(i)
                    dep = (list(start_deps) + [tokv]) if i == 0 else []
                    sc_toks.append(sch.add("scalar", dep,
                                   lambda s, s2=s2, ci=ci, i=i, uw7=uw7:
                                   s.activation(uw7[:, i * 256:i * 256 + 128],
                                                s2[:, 128:256], AF.Copy,
                                                scale=col("B", 1, ci))))
                    sc_toks.append(sch.add("scalar", [],
                                   lambda s, s2=s2, ci=ci, i=i, uw7=uw7:
                                   s.activation(uw7[:, i * 256 + 128:
                                                    (i + 1) * 256],
                                                s2[:, 0:128], AF.Copy,
                                                scale=col("B", 1, ci))))
                for i in range(7):
                    ci = r * 7 + i
                    s2 = s2_of(i)
                    sch.add("vector", [sc_toks[2 * i]],
                            lambda v, s2=s2, ci=ci, i=i, uw7=uw7:
                            v.scalar_tensor_tensor(
                                uw7[:, i * 256:i * 256 + 128], s2[:, 0:128],
                                col("A", 1, ci),
                                uw7[:, i * 256:i * 256 + 128],
                                OP.mult, OP.add))
                    sch.add("vector", [sc_toks[2 * i + 1]],
                            lambda v, s2=s2, ci=ci, i=i, uw7=uw7:
                            v.scalar_tensor_tensor(
                                uw7[:, i * 256 + 128:(i + 1) * 256],
                                s2[:, 128:256], col("C", 1, ci),
                                uw7[:, i * 256 + 128:(i + 1) * 256],
                                OP.mult, OP.add))
                for i in range(7):
                    sch.add("vector", [], lambda v, i=i, dst=dst, uw7=uw7:
                            v.tensor_tensor(
                                dst[:, i * 256:(i + 1) * 256].rearrange(
                                    "p (a b) -> p a b", a=2),
                                uw7[:, i * 256:(i + 1) * 256].rearrange(
                                    "p (a b) -> p a b", a=2),
                                scrc7[:, i * 128:(i + 1) * 128].unsqueeze(
                                    1).broadcast_to([128, 2, 128]),
                                OP.mult))
                for i in range(7):
                    s2 = s2_of(i)
                    tokv = sch.add("vector", [], lambda v, i=i, dst=dst, s2=s2:
                                   v.tensor_tensor(dst[:, i * 256:(i + 1) * 256],
                                                   dst[:, i * 256:(i + 1) * 256],
                                                   s2, OP.add))
            return tokv

        def emit_a2_tile(i):
            qsb = pa("qsb0" if i % 2 == 0 else "qsb1", QKVD)
            dq = sch.dma([a2_state["qcons"][i % 2],
                          a2_state.get("qcons_ln"),
                          sch.dma_total("qp0"), sch.dma_total("qp1")] +
                         a2_state.get("dbgguard", []),
                         lambda b, i=i, qsb=qsb:
                         b.dma_start(qsb, qkv_park[i * 128:(i + 1) * 128, :]),
                         slot="qi%d" % (i % 2))
            sq = pa("sq", QKVD)
            t_sq = sch.add("scalar", [dq, a2_state.get("sqguard")],
                           lambda s: s.activation(sq, qsb, AF.Square))
            t_ss = sch.add("vector", [t_sq], lambda v: v.tensor_reduce(
                nrmc(0, 24), sq.rearrange("p (h d) -> p h d", h=24), AX.X, OP.add))
            t_mx = sch.add("vector", [t_ss], lambda v: v.tensor_scalar(
                nrmc(0, 24), nrmc(0, 24), 1e-24, None, OP.max))
            t_ln = sch.add("scalar", [t_mx], lambda s: s.activation(
                nrmc(0, 24), nrmc(0, 24), AF.Ln))
            t_rn = sch.add("scalar", [t_ln], lambda s: s.activation(
                nrmc(0, 24), nrmc(0, 24), AF.Exp, scale=-0.5))
            t_rn2 = sch.add("vector", [t_rn], lambda v: v.tensor_tensor(
                nrmc(40, 16), nrmc(0, 16), nrmc(0, 16), OP.mult))
            enc_toks = []
            for z, key in ((0, "qe"), (1, "ke"), (2, "vq")):
                for h in range(8):
                    src = qsb[:, (z * 8 + h) * 128:(z * 8 + h) * 128 + 128]
                    for ci, trig in ((0, "cos"), (1, "sin")):
                        def ef(v, src=src, key=key, h=h, ci=ci, trig=trig, z=z):
                            return v.scalar_tensor_tensor(
                                pa(key, 2048)[:, h * 256 + ci * 128:
                                              h * 256 + ci * 128 + 128],
                                src, nrmc(z * 8 + h), cv(trig, 128),
                                OP.mult, OP.mult)
                        enc_toks.append(sch.add("vector", [t_rn], ef))
            t_enc = enc_toks[-1]
            tq = entangle_state("qe", 0, [t_enc])
            tk = entangle_state("ke", 1024, [t_enc, tq])
            t_vs = sch.add("vector", [t_enc], lambda v: v.tensor_reduce(
                pa("vsum", 256).rearrange("p (c d) -> p c d", c=2),
                pa("vq", 2048).rearrange("p (j c d) -> p c d j", j=8, c=2),
                AX.X, OP.add))
            t_m2 = sch.add("scalar", [t_vs], lambda s: s.activation(
                pa("scr2", 256), pa("vsum", 256), AF.Square))
            t_m2b = sch.add("vector", [t_m2], lambda v: v.tensor_tensor(
                pa("m2c", 128), pa("scr2", 256)[:, 0:128],
                pa("scr2", 256)[:, 128:256], OP.add))
            t_m2c = sch.add("vector", [t_m2b], lambda v: v.tensor_scalar(
                pa("m2c", 128), pa("m2c", 128), 1e-30, None, OP.max))
            t_li = sch.add("scalar", [t_m2c], lambda s: s.activation(
                pa("rinv", 128), pa("m2c", 128), AF.Ln))
            t_ri = sch.add("scalar", [t_li], lambda s: s.activation(
                pa("rinv", 128), pa("rinv", 128), AF.Exp, scale=-0.5))
            t_coh = sch.add("vector", [t_ri], lambda v: v.tensor_tensor(
                pa("coh", 256).rearrange("p (c d) -> p c d", c=2),
                pa("vsum", 256).rearrange("p (c d) -> p c d", c=2),
                pa("rinv", 128).unsqueeze(1).broadcast_to([128, 2, 128]),
                OP.mult))
            gsc = pa("gsc", 2048)
            gsci = PF[:, A2["m2w"]:A2["m2w"] + 2048]
            qe4 = pa("qe", 2048).rearrange("p (i c d) -> p i c d", i=8, c=2)
            gv = gsc.rearrange("p (i c d) -> p i c d", i=8, c=2)
            gvi = gsci.rearrange("p (i c d) -> p i c d", i=8, c=2)
            t_gram = [tq, tk]
            g3 = g6 = None
            for j in range(8):
                kr = pa("ke", 2048)[:, j * 256:j * 256 + 128]
                ki = pa("ke", 2048)[:, j * 256 + 128:j * 256 + 256]
                krb = kr.unsqueeze(1).broadcast_to([128, 8, 128])
                kib = ki.unsqueeze(1).broadcast_to([128, 8, 128])
                g1 = sch.add("vector", t_gram if j == 0 else [], lambda v, krb=krb:
                             v.tensor_tensor(gv[:, :, 0, :], qe4[:, :, 0, :],
                                             krb, OP.mult))
                g2 = sch.add("vector", [], lambda v, kib=kib: v.tensor_tensor(
                    gv[:, :, 1, :], qe4[:, :, 1, :], kib, OP.mult))
                g3 = sch.add("vector", [], lambda v, j=j: v.tensor_reduce(
                    pa("P", 128)[:, j * 8:j * 8 + 8],
                    gsc.rearrange("p (i e) -> p i e", i=8), AX.X, OP.add))
                g4 = sch.add("gpsimd", (t_gram + [a2_state.get("grguard")])
                             if j == 0 else [g6], lambda g, kib=kib:
                             g.tensor_tensor(gvi[:, :, 0, :], qe4[:, :, 0, :],
                                             kib, OP.mult))
                g5 = sch.add("gpsimd", [], lambda g, krb=krb:
                             g.tensor_tensor(gvi[:, :, 1, :], qe4[:, :, 1, :],
                                             krb, OP.mult))
                g6a = sch.add("vector", [g5], lambda v: v.tensor_reduce(
                    pa("scr2", 256)[:, 0:16].rearrange("p (i c) -> p i c", c=2),
                    gsci.rearrange("p (i c d) -> p i c d", i=8, c=2),
                    AX.X, OP.add))
                g6 = sch.add("vector", [g6a], lambda v, j=j: v.tensor_tensor(
                    pa("P", 128)[:, 64 + j * 8:64 + j * 8 + 8],
                    pa("scr2", 256)[:, 0:16].rearrange(
                        "p (i c) -> p i c", c=2)[:, :, 0],
                    pa("scr2", 256)[:, 0:16].rearrange(
                        "p (i c) -> p i c", c=2)[:, :, 1],
                    OP.subtract))
            t_P = sch.add("vector", [g3, g6], lambda v: v.memset(consts[0:1, CO["rows"] + 4200:CO["rows"] + 4201], 0.0))
            t_s2a = sch.add("scalar", [t_P], lambda s: s.activation(
                pa("s2", 64), pa("P", 128)[:, 0:64], AF.Square))
            t_s2b = sch.add("scalar", [t_s2a], lambda s: s.activation(
                pa("ec", 64), pa("P", 128)[:, 64:128], AF.Square))
            t_s2 = sch.add("vector", [t_s2b], lambda v: v.tensor_tensor(
                pa("s2", 64), pa("s2", 64), pa("ec", 64), OP.add))
            t_fz = sch.add("gpsimd", [a2_state.get("finguard")] +
                           a2_state.pop("dbgguard", []),
                           lambda g: g.memset(pa("fin", 2048), 0.0))
            t_fin = t_fz
            ECB = A2["pp1"] + 256
            WTP = [A2["wt"], A2["pp0"]]

            def pec(c):
                return PF[:, ECB + c * 64:ECB + (c + 1) * 64]

            def pden(c):
                return PF[:, ECB + 256 + c * 16:ECB + 256 + (c + 1) * 16]

            def pwm(c):
                return PF[:, ECB + 320 + c * 64:ECB + 320 + (c + 1) * 64]

            def pwt(c):
                return PF[:, WTP[c % 2]:WTP[c % 2] + 2048]

            sm_toks = {}

            def emit_sm(cix):
                t_e = sch.add("scalar", [t_s2, t_bcgam], lambda s, cix=cix:
                              s.activation(pec(cix), pa("s2", 64), AF.Exp,
                                           scale=col("gam", 1, cix)))
                t_dn = sch.add("vector", [t_e], lambda v, cix=cix:
                               v.tensor_reduce(
                    pden(cix)[:, 0:8],
                    pec(cix).rearrange("p (j i) -> p i j", j=8), AX.X, OP.add))
                t_rd = sch.add("vector", [t_dn], lambda v, cix=cix:
                               v.reciprocal(pden(cix)[:, 8:16],
                                            pden(cix)[:, 0:8]))
                t_pr = sch.add("vector", [t_rd], lambda v, cix=cix:
                               v.tensor_tensor(
                    pwm(cix).rearrange("p (j i) -> p j i", j=8),
                    pec(cix).rearrange("p (j i) -> p j i", j=8),
                    pden(cix)[:, 8:16].unsqueeze(1).broadcast_to([128, 8, 8]),
                    OP.mult))
                t_pr2 = sch.add("vector", [t_pr], lambda v, cix=cix:
                                v.tensor_scalar(pwm(cix), pwm(cix), 1e-12,
                                                None, OP.add))
                t_lnp = sch.add("scalar", [t_pr2], lambda s, cix=cix:
                                s.activation(pwm(cix), pwm(cix), AF.Ln))
                t_w = sch.add("scalar", [t_lnp], lambda s, cix=cix:
                              s.activation(pwm(cix), pwm(cix), AF.Exp,
                                           scale=0.5))
                ap_toks = []
                for ii in range(8):
                    tok = sch.add("vector", [t_w], lambda v, ii=ii, cix=cix:
                                  v.tensor_scalar(
                                      pwt(cix)[:, ii * 256:(ii + 1) * 256],
                                      pa("vq", 2048)[:, 0:256],
                                      pwm(cix)[:, ii:ii + 1],
                                      None, OP.mult))
                    for j in range(1, 8):
                        tok = sch.add("vector", [tok],
                                      lambda v, ii=ii, j=j, cix=cix:
                                      v.scalar_tensor_tensor(
                                          pwt(cix)[:, ii * 256:(ii + 1) * 256],
                                          pa("vq", 2048)[:, j * 256:(j + 1) * 256],
                                          pwm(cix)[:, j * 8 + ii:j * 8 + ii + 1],
                                          pwt(cix)[:, ii * 256:(ii + 1) * 256],
                                          OP.mult, OP.add))
                    ap_toks.append(tok)
                sm_toks[cix] = (t_e, t_rd, t_w, ap_toks)

            def emit_st6(cix, t_fin, gsc_guard):
                t_ap = sm_toks[cix][3]
                t_q2 = sch.add("scalar", t_ap + [gsc_guard],
                               lambda s, cix=cix: s.activation(
                                   pa("gsc", 2048), pwt(cix), AF.Square))
                wt4 = pa("gsc", 2048).rearrange("p (i c d) -> p i c d", i=8, c=2)
                t_m2w = sch.add("vector", [t_q2], lambda v: v.tensor_tensor(
                    pa("m2w", 1024).rearrange("p (i d) -> p i d", i=8),
                    wt4[:, :, 0, :], wt4[:, :, 1, :], OP.add))
                t_m2x = sch.add("vector", [t_m2w], lambda v: v.tensor_scalar(
                    pa("m2w", 1024), pa("m2w", 1024), 1e-30, None, OP.max))
                t_l6 = sch.add("scalar", [t_m2x], lambda s: s.activation(
                    pa("rr6", 1024), pa("m2w", 1024), AF.Ln))
                t_r6 = sch.add("scalar", [t_l6], lambda s: s.activation(
                    pa("rr6", 1024), pa("rr6", 1024), AF.Exp, scale=-0.5))
                t_f1 = sch.add("vector", [t_r6], lambda v: v.tensor_scalar(
                    pa("fac", 1024), pa("rr6", 1024), 1.0, None, OP.min))
                t_f2 = sch.add("vector", [t_f1], lambda v: v.scalar_tensor_tensor(
                    pa("fac", 1024), pa("rr6", 1024), 1e-6, pa("fac", 1024),
                    OP.mult, OP.max))
                t_fs = sch.add("vector", [t_f2, t_bcsup], lambda v, cix=cix:
                               v.tensor_scalar(pa("fac", 1024), pa("fac", 1024),
                                               col("supw", 1, cix), None,
                                               OP.mult))
                fs4 = pa("fac", 1024).rearrange("p (i d) -> p i d", i=8
                                                ).unsqueeze(2).broadcast_to(
                                                    [128, 8, 2, 128])
                t_tm = sch.add("vector", [t_fs], lambda v, cix=cix:
                               v.tensor_tensor(
                    pa("gsc", 2048).rearrange("p (i c d) -> p i c d", i=8, c=2),
                    pwt(cix).rearrange("p (i c d) -> p i c d", i=8, c=2),
                    fs4, OP.mult))
                t_fin = sch.add("vector", [t_tm, t_fin] + findg[:], lambda v:
                                v.tensor_tensor(pa("fin", 2048),
                                                pa("fin", 2048),
                                                pa("gsc", 2048), OP.add))
                del findg[:]
                if i == 0 and dbg_out("finc%d" % cix, (128, 2048)):
                    dtok = sch.dma([t_fin], lambda b, cix=cix: b.dma_start(
                        dbg["finc%d" % cix], pa("fin", 2048)), slot="dbg1")
                    findg.append(dtok)
                    a2_state.setdefault("dbgguard", []).append(dtok)
                return t_fin, t_fin

            emit_sm(0)
            gsc_guard = None
            findg = []
            for cix in range(4):
                if cix < 3:
                    emit_sm(cix + 1)
                t_fin, gsc_guard = emit_st6(cix, t_fin, gsc_guard)
            if i == 0:
                for nm, src, dtok in (("c0ec", pec(0), sm_toks[0][0]),
                                      ("c0den", pden(0), sm_toks[0][1]),
                                      ("c0w", pwm(0), sm_toks[0][2]),
                                      ("cgam", col("gam", 4), t_bcgam),
                                      ("csup", col("supw", 4), t_bcsup)):
                    if dbg_out(nm, (128, src.shape[-1])):
                        a2_state.setdefault("dbgguard", []).append(
                            sch.dma([dtok], lambda b, nm=nm, src=src:
                                    b.dma_start(dbg[nm], src), slot="dbg1"))
            fin4 = pa("fin", 2048).rearrange("p (i c d) -> p i c d", i=8, c=2)
            cr = pa("coh", 256)[:, 0:128].unsqueeze(1).broadcast_to([128, 8, 128])
            cib = pa("coh", 256)[:, 128:256].unsqueeze(1).broadcast_to([128, 8, 128])
            t_o1 = sch.add("vector", [t_fin, t_coh], lambda v: v.tensor_tensor(
                pa("m1s", 1024).rearrange("p (i d) -> p i d", i=8),
                fin4[:, :, 0, :], cr, OP.mult))
            t_o2 = sch.add("vector", [t_o1], lambda v: v.tensor_tensor(
                pa("m2s", 1024).rearrange("p (i d) -> p i d", i=8),
                fin4[:, :, 1, :], cib, OP.mult))
            t_out = sch.add("vector", [t_o2], lambda v: v.tensor_tensor(
                pa("outr", 1024), pa("m1s", 1024), pa("m2s", 1024), OP.subtract))
            if i == 0 and dbg_out("a2rn", (128, 64)):
                a2_state.setdefault("dbgguard", []).append(
                    sch.dma([t_rn2], lambda b: b.dma_start(
                        dbg["a2rn"], PF[:, A2["nrm"]:A2["nrm"] + 64]),
                        slot="dbg1"))
            if i == 0 and dbg_out("trig", (128, 256)):
                a2_state.setdefault("dbgguard", []).append(
                    sch.dma([t_enc], lambda b: b.dma_start(
                        dbg["trig"], consts[:, CO["cos"]:CO["cos"] + 256]),
                        slot="dbg1"))
            if i == 0:
                for nm, src, dtok in (("a2qe", pa("qe", 2048), tq),
                                      ("a2ke", pa("ke", 2048), tk),
                                      ("a2vq", pa("vq", 2048), t_enc),
                                      ("a2coh", pa("coh", 256), t_coh),
                                      ("a2P", pa("P", 128), t_P),
                                      ("a2fin", pa("fin", 2048), t_fin),
                                      ("a2out", pa("outr", 1024), t_out)):
                    if dbg_out(nm, (128, src.shape[-1])):
                        a2_state.setdefault("dbgguard", []).append(
                            sch.dma([dtok], lambda b, nm=nm, src=src:
                                    b.dma_start(dbg[nm], src), slot="dbg1"))
            ev = None
            for kc in range(8):
                ev = emit_tp(pa("outr", 1024)[:, kc * 128:(kc + 1) * 128],
                             PB[:, XF2 + kc * 128:XF2 + kc * 128 + 128],
                             PB[:, XF2 + 1024 + kc * 128:XF2 + 1024 + kc * 128 + 128],
                             [t_out, a2_state.get("xf2guard")])
            dx2 = sch.dma([a2_state["xb2cons"], t_ss], lambda b, i=i: b.dma_start(
                pa("xb2", 1024), x_hbm[i * 128:(i + 1) * 128, :]), slot="ax2")
            x2parts = []
            for nch in range(2):
                bank = nch

                def mmo(t, nch=nch, bank=bank):
                    last = None
                    for kc in range(8):
                        xh = PB[:, XF2 + kc * 128:XF2 + kc * 128 + 128]
                        xl = PB[:, XF2 + 1024 + kc * 128:XF2 + 1024 + kc * 128 + 128]
                        wh = PB[:, kc * 1024 + nch * 512:kc * 1024 + nch * 512 + 512]
                        wl = PB[:, 8192 + kc * 1024 + nch * 512:
                                8192 + kc * 1024 + nch * 512 + 512]
                        t.matmul(pb[bank], xh, wh, start=(kc == 0), stop=False)
                        t.matmul(pb[bank], xh, wl, start=False, stop=False)
                        last = t.matmul(pb[bank], xl, wh, start=False, stop=False)
                    return last
                mtok = pe_w(bank, [ev, t_wo], mmo)
                btok = sch.add("tensor", [d_bo, t_ones], lambda t, nch=nch, bank=bank:
                               t.matmul(pb[bank], ones_row,
                                        bout_row[:, nch * 512:(nch + 1) * 512],
                                        start=False, stop=True))
                bank_w[bank] = btok
                a2_state["xf2guard"] = btok
                tga = rd_bank("vector", bank, [dx2], lambda v, nch=nch, bank=bank:
                              v.tensor_tensor(pa("m1s", 1024)[:, nch * 512:
                                                              (nch + 1) * 512],
                                              pb[bank],
                                              cv("ga", 1024)[:, nch * 512:
                                                             (nch + 1) * 512],
                                              OP.mult))
                tx2 = sch.add("vector", [tga], lambda v, nch=nch, i=i:
                              v.tensor_tensor(
                                  pa("qsb0" if i % 2 == 0 else "qsb1", QKVD)
                                  [:, nch * 512:(nch + 1) * 512],
                                  pa("m1s", 1024)[:, nch * 512:(nch + 1) * 512],
                                  pa("xb2", 1024)[:, nch * 512:(nch + 1) * 512],
                                  OP.add))
                x2parts.append(tx2)
            a2_state["xb2cons"] = x2parts[-1]
            a2_state["sqguard"] = x2parts[-1]
            a2_state["finguard"] = t_out
            a2_state["grguard"] = t_out
            x2v = pa("qsb0" if i % 2 == 0 else "qsb1", QKVD)[:, 0:1024]
            jnk = pa("m2w", 1024)
            l1 = sch.add("scalar", x2parts, lambda s: s.activation(
                jnk, x2v, AF.Copy, accum_out=nrmc(60)))
            l2 = sch.add("scalar", [l1], lambda s: s.activation(
                nrmc(61), nrmc(60), AF.Copy, scale=-1.0 / D))
            l3 = sch.add("scalar", [l2], lambda s: s.activation(
                jnk, x2v, AF.Square, bias=nrmc(61), accum_out=nrmc(62)))
            l4 = sch.add("scalar", [l3], lambda s: s.activation(
                nrmc(63), nrmc(62), AF.Copy, scale=1.0 / D, bias=1e-5))
            l5 = sch.add("scalar", [l4], lambda s: s.activation(
                nrmc(63), nrmc(63), AF.Ln))
            l6 = sch.add("scalar", [l5], lambda s, i=i: s.activation(
                cv("b_ln", 128)[:, 2 * i:2 * i + 1], nrmc(63), AF.Exp,
                scale=-0.5))
            l7 = sch.add("vector", [l6], lambda v, i=i: v.tensor_scalar(
                cv("b_ln", 128)[:, 2 * i + 1:2 * i + 2], nrmc(61),
                cv("b_ln", 128)[:, 2 * i:2 * i + 1], None, OP.mult))
            a2_lnstash.append(l7)
            dx2p = sch.dma(x2parts + [l3], lambda b, i=i: b.dma_start(
                x2_park[i * 128:(i + 1) * 128, :],
                pa("qsb0" if i % 2 == 0 else "qsb1", QKVD)[:, 0:1024]),
                slot="xp%d" % (i % 2))
            a2_parks.append(dx2p)
            a2_state["qcons"][i % 2] = dx2p
            a2_state["qcons_ln"] = l3
            return dx2p

        last_a2 = None
        for i in (range(ntiles) if _lvl >= 4 else ()):
            last_a2 = emit_a2_tile(i)

        if dbg_out("x2", (TOK, D)):
            sch.dma([last_a2, sch.dma_total("xp0"), sch.dma_total("xp1")],
                    lambda b: b.dma_start(dbg["x2"], x2_park), slot="dbg4")

        B = dict(acc=0, x2b=4096, h2=5120, lnc=6144, junk=6160,
                 qzn=7184, qaz=8208, qacc=9232, qsg=10256, qam=11280, qrs=11344,
                 sgs=11504, hch=11760, z2=12016, a2_=12272, c2=12528, s2_=12784,
                 m2=13040, r2=13056, yt=13072)
        bq_scr = {"zn": PF[:, B["qzn"]:B["qzn"] + 1024],
                  "az": PF[:, B["qaz"]:B["qaz"] + 1024],
                  "acc": PF[:, B["qacc"]:B["qacc"] + 1024],
                  "sg": PF[:, B["qsg"]:B["qsg"] + 1024],
                  "am": PF[:, B["qam"]:B["qam"] + 64],
                  "rs": PF[:, B["qrs"]:B["qrs"] + 64]}
        bh_scr = {"zn": PF[:, B["z2"]:B["z2"] + 256],
                  "az": PF[:, B["a2_"]:B["a2_"] + 256],
                  "acc": PF[:, B["c2"]:B["c2"] + 256],
                  "sg": PF[:, B["s2_"]:B["s2_"] + 256],
                  "am": PF[:, B["m2"]:B["m2"] + 16],
                  "rs": PF[:, B["r2"]:B["r2"] + 16]}

        def pbv(off, n):
            return PB[:, off:off + n]

        def blnc(i_):
            return PF[:, B["lnc"] + i_:B["lnc"] + i_ + 1]

        B2 = dict(q4zn=14336, q4az=15360, q4acc=16384, q4sg=17408,
                  q4am=18432, q4rs=18496, hch4=18560, sgs2=19584,
                  hch4b=20096)
        bh4_scr = {"zn": PF[:, B2["q4zn"]:B2["q4zn"] + 1024],
                   "az": PF[:, B2["q4az"]:B2["q4az"] + 1024],
                   "acc": PF[:, B2["q4acc"]:B2["q4acc"] + 1024],
                   "sg": PF[:, B2["q4sg"]:B2["q4sg"] + 1024],
                   "am": PF[:, B2["q4am"]:B2["q4am"] + 64],
                   "rs": PF[:, B2["q4rs"]:B2["q4rs"] + 64]}
        b_state = {"x2c": last_a2, "wc": last_a2, "hq4": {},
                   "sgsg": [None, None], "hchg2": [[], []]}
        NCH = FF // 256
        last_y = None
        if _lvl >= 5:
            t_preb = sch.add("scalar", [], lambda s: s.preload_act_table(AF.Silu))
            for _ in range(3):
                t_preb = sch.add("scalar", [t_preb], lambda s: s.activation(
                    dummy_row, dummy_row, AF.Silu))
        for g in (range(NG) if _lvl >= 5 else ()):
            t_az = sch.add("gpsimd", [b_state["x2c"]], lambda g_: g_.memset(
                PF[:, B["acc"]:B["acc"] + 4096], 0.0))
            headtoks = []
            for j in range(4):
                ti = g * 4 + j
                dxt = sch.dma([b_state["x2c"],
                               a2_parks[ti] if _lvl >= 4 else None],
                              lambda b, ti=ti:
                              b.dma_start(PF[:, B["x2b"]:B["x2b"] + 1024],
                                          x2_park[ti * 128:(ti + 1) * 128, :]),
                              slot="bx")
                xb = PF[:, B["x2b"]:B["x2b"] + 1024]
                junk = PF[:, B["junk"]:B["junk"] + 1024]
                h2 = PF[:, B["h2"]:B["h2"] + 1024]
                s8 = sch.add("vector", [dxt, a2_lnstash[ti] if _lvl >= 4
                                        else None],
                             lambda v, ti=ti: v.tensor_scalar(
                                 junk, xb, cv("b_ln", 128)[:, 2 * ti:2 * ti + 1],
                                 cv("b_ln", 128)[:, 2 * ti + 1:2 * ti + 2],
                                 OP.mult, OP.add))
                s9 = sch.add("vector", [s8], lambda v: v.tensor_tensor(
                    h2, junk, cv("cb1", 1024), OP.mult))
                sA = sch.add("vector", [s9], lambda v: v.tensor_tensor(
                    h2, h2, cv("sb_", 1024), OP.add))
                b_state["x2c"] = sA
                qt = emit_quant(h2, h2, 1024, bq_scr, [sA])
                ev = None
                for kc in range(8):
                    ho = j * 1024 + kc * 128
                    ev = emit_tp(h2[:, kc * 128:(kc + 1) * 128],
                                 pbv(ho, 128), pbv(4096 + ho, 128), [qt])
                headtoks.append(ev)
            t_head = sch.add("vector", headtoks, lambda v: v.memset(consts[0:1, CO["rows"] + 4200:CO["rows"] + 4201], 0.0))
            wq_done = [sch.dma_total("wqp0"), sch.dma_total("wqp1")] \
                if _lvl >= 2 else []

            def wload(b, ncix, key, dsto, hl):
                park = wq_park[key]
                NTOT = FF if key != "d" else D
                base = hl * D * FF + ncix * 256
                return b.dma_start(
                    pbv(dsto + hl * 2048, 2048).rearrange(
                        "p (kc e) -> p kc e", kc=8),
                    bass.AP(park.tensor, base,
                            [[NTOT, 128], [128 * NTOT, 8], [1, 256]]))

            def emit_dloads(ncix, deps):
                toks = []
                for hl in (0, 1):
                    def wdl(b, ncix=ncix, hl=hl):
                        base = hl * FF * D + ncix * 256 * D
                        return b.dma_start(
                            pbv(16384 + hl * 2048, 2048).rearrange(
                                "p (kq n) -> p kq n", kq=2),
                            bass.AP(wq_park["d"].tensor, base,
                                    [[D, 128], [128 * D, 2], [1, 1024]]))
                    toks.append(sch.dma(deps + wq_done, wdl,
                                        slot="bw%d" % (4 + hl)))
                return toks

            def emit_gu(ncix, first, t_head):
                wr_prev = b_state.get("wr_gu") or [b_state["wc"]]
                dws = []
                for si, (key, dsto) in enumerate((("g", 8192), ("u", 12288))):
                    for hl in (0, 1):
                        dws.append(sch.dma(
                            wr_prev + wq_done,
                            lambda b, key=key, dsto=dsto, hl=hl, ncix=ncix:
                            wload(b, ncix, key, dsto, hl),
                            slot="bw%d" % (si * 2 + hl)))
                if first:
                    b_state["dw_d"] = emit_dloads(
                        ncix, b_state.get("dread") or [b_state["wc"]])
                hch = PF[:, B2["hch4" if ncix % 2 == 0 else "hch4b"]:
                         B2["hch4" if ncix % 2 == 0 else "hch4b"] + 1024]
                th_toks = []
                mgmu = []
                for j in range(4):
                    def mmgu(t, j=j, base=8192):
                        last = None
                        for kc in range(8):
                            xh = pbv(j * 1024 + kc * 128, 128)
                            xl = pbv(4096 + j * 1024 + kc * 128, 128)
                            wh = pbv(base + kc * 256, 256)
                            wl = pbv(base + 2048 + kc * 256, 256)
                            t.matmul(pb[0 if base == 8192 else 1][:, 0:256],
                                     xh, wh, start=(kc == 0), stop=False)
                            t.matmul(pb[0 if base == 8192 else 1][:, 0:256],
                                     xh, wl, start=False, stop=False)
                            last = t.matmul(pb[0 if base == 8192 else 1][:, 0:256],
                                            xl, wh, start=False,
                                            stop=(kc == 7))
                        return last
                    mg = pe_w(0, [t_head] + dws, lambda t, j=j: mmgu(t, j, 8192))
                    mu = pe_w(1, [t_head] + dws, lambda t, j=j: mmgu(t, j, 12288))
                    sgs_j = PF[:, B2["sgs2"] + (j % 2) * 256:
                               B2["sgs2"] + (j % 2) * 256 + 256]
                    tsg = rd_bank("scalar", 0, [b_state["sgsg"][j % 2]],
                                  lambda s, sgs_j=sgs_j: s.activation(
                                      sgs_j, pb[0][:, 0:256], AF.Silu))
                    th = rd_bank("vector", 1,
                                 [tsg] + b_state["hchg2"][ncix % 2],
                                 lambda v, j=j, sgs_j=sgs_j, hch=hch:
                                 v.tensor_tensor(hch[:, j * 256:(j + 1) * 256],
                                                 sgs_j, pb[1][:, 0:256],
                                                 OP.mult))
                    b_state["sgsg"][j % 2] = th
                    th_toks.append(th)
                    mgmu.append(mg)
                    mgmu.append(mu)
                b_state["wr_gu"] = mgmu
                qh = emit_quant(hch, hch, 1024, bh4_scr, th_toks)
                return dict(qh=qh, hch=hch, ncix=ncix)

            def emit_tail(st, next_ncix, t_az):
                ncix, qh, hch = st["ncix"], st["qh"], st["hch"]
                hev = {}
                for j in range(4):
                    for kq in range(2):
                        lt = emit_tp(hch[:, j * 256 + kq * 128:
                                          j * 256 + kq * 128 + 128],
                                     pbv(20480 + j * 512 + kq * 128, 128),
                                     pbv(22528 + j * 512 + kq * 128, 128),
                                     [qh] + b_state["hq4"].get(j, []))
                        hev.setdefault(j, []).append(lt)
                b_state["hchg2"][ncix % 2] = [hev[j][1] for j in range(4)]
                dtoks = b_state["dw_d"]
                mds = []
                last_tacc = None
                for j in range(4):
                    for half in range(2):
                        bank = 2 + half

                        def mmd(t, j=j, half=half, bank=bank):
                            last = None
                            for kq in range(2):
                                xh = pbv(20480 + j * 512 + kq * 128, 128)
                                xl = pbv(22528 + j * 512 + kq * 128, 128)
                                wh = pbv(16384 + kq * 1024 + half * 512, 512)
                                wl = pbv(18432 + kq * 1024 + half * 512, 512)
                                t.matmul(pb[bank], xh, wh, start=(kq == 0),
                                         stop=False)
                                t.matmul(pb[bank], xh, wl, start=False,
                                         stop=False)
                                last = t.matmul(pb[bank], xl, wh, start=False,
                                                stop=(kq == 1))
                            return last
                        md = pe_w(bank, hev[j] + dtoks, mmd)
                        last_tacc = rd_bank(
                            "vector", bank, [t_az],
                            lambda v, j=j, half=half, bank=bank:
                            v.scalar_tensor_tensor(
                                PF[:, B["acc"] + j * 1024 + half * 512:
                                   B["acc"] + j * 1024 + half * 512 + 512],
                                pb[bank], 1.0,
                                PF[:, B["acc"] + j * 1024 + half * 512:
                                   B["acc"] + j * 1024 + half * 512 + 512],
                                OP.mult, OP.add))
                        mds.append(md)
                    b_state["hq4"][j] = [md]
                b_state["dread"] = mds
                b_state["wc"] = last_tacc
                if next_ncix is not None:
                    b_state["dw_d"] = emit_dloads(next_ncix, mds)

            st = emit_gu(0, True, t_head)
            for c in range(NCH):
                nst = emit_gu(c + 1, False, t_head) if c + 1 < NCH else None
                emit_tail(st, c + 1 if c + 1 < NCH else None, t_az)
                st = nst
            for j in range(4):
                ti = g * 4 + j
                dxt = sch.dma([b_state["x2c"],
                               a2_parks[ti] if _lvl >= 4 else None],
                              lambda b, ti=ti: b.dma_start(
                                  PF[:, B["x2b"]:B["x2b"] + 1024],
                                  x2_park[ti * 128:(ti + 1) * 128, :]),
                              slot="bx")
                tm = sch.add("vector", [dxt, b_state["wc"]], lambda v, j=j:
                             v.tensor_tensor(PF[:, B["junk"]:B["junk"] + 1024],
                                             PF[:, B["acc"] + j * 1024:
                                                B["acc"] + (j + 1) * 1024],
                                             cv("gb", 1024), OP.mult))
                ty = sch.add("vector", [tm], lambda v: v.tensor_tensor(
                    PF[:, B["yt"]:B["yt"] + 1024],
                    PF[:, B["junk"]:B["junk"] + 1024],
                    PF[:, B["x2b"]:B["x2b"] + 1024], OP.add))
                b_state["x2c"] = ty
                last_y = sch.dma([ty, last_y], lambda b, ti=ti: b.dma_start(
                    y_hbm[ti * 128:(ti + 1) * 128, :],
                    PF[:, B["yt"]:B["yt"] + 1024]), slot="by")

        sch.add("sync", sch.all_dma_done(), lambda b: None)

        slot_sems = {s: ctx.enter_context(nc.semaphore("ds_" + s))
                     for s in sch.slot_count}
        block = ctx.enter_context(nc.Block())
        sch.emit(block, sems, slot_sems)

    return nc, dbg


def make_in_map(core, inputs, ntiles=32):
    TOK = ntiles * 128
    b = core // 2
    half = core % 2
    x = np.ascontiguousarray(
        inputs["L0_attn_query"][b, half * TOK:(half + 1) * TOK, :])
    return {
        "x": x,
        "wqkv": inputs["L0_attn_W_qkv"],
        "bqkv": np.ascontiguousarray(inputs["L0_attn_b_qkv"]).reshape(1, -1),
        "wout": inputs["L0_attn_W_out"],
        "bout": np.ascontiguousarray(inputs["L0_attn_b_out"]).reshape(1, -1),
        "qw00": np.ascontiguousarray(
            inputs["L0_attn_quantum_weights"][:, 0, 0]).reshape(1, 4),
        "phases": inputs["L0_attn_photonic_phases"],
        "gates": np.ascontiguousarray(
            inputs["L0_attn_entanglement_gates"]).reshape(1, 96),
        "sup": np.ascontiguousarray(
            inputs["L0_attn_superposition_logits"]).reshape(1, 4),
        "wgate": inputs["L0_ffn_w_gate"],
        "wup": inputs["L0_ffn_w_up"],
        "wdown": inputs["L0_ffn_w_down"],
        "c_col": np.ascontiguousarray(inputs["L0_c"][b]).reshape(D, 1),
        "wmod": inputs["L0_wmod"],
    }


def kernel(**inputs):
    ntiles = 32
    nc, _ = build_program(ntiles)
    in_maps = [make_in_map(c, inputs, ntiles) for c in range(8)]
    res = run_bass_kernel_spmd(nc, in_maps, core_ids=list(range(8)))
    TOK = ntiles * 128
    out = np.empty((4, 8192, D), dtype=np.float32)
    for c in range(8):
        b, half = c // 2, c % 2
        out[b, half * TOK:(half + 1) * TOK, :] = res.results[c]["y"]
    return out
